# version 10
# baseline (speedup 1.0000x reference)
import numpy as np
import ml_dtypes
import concourse.bass as bass
import concourse.mybir as mybir
from concourse.bass_utils import run_bass_kernel_spmd

F32 = mybir.dt.float32
BF16 = mybir.dt.bfloat16
AF = mybir.ActivationFunctionType
ALU = mybir.AluOpType
AX = mybir.AxisListType

HID = 2048
GATE = 4096
CONVD = 4352
INTER = 4096
HEADS = 32
CH = 256
NCH = 4
TT = 8
TLOC = 1024
TPAD = 1152
NCT = 34
KH = 16
EPS_LN = 1e-5
EPS_RMS = 1e-5
EPS_MOE = 1e-8
NE = 128
KE = 8
FI = 1024
H2 = 2048
BIG = 1.0e30
NDMASEM = 20
NDSSEM = 8
SEMS = (("pe", "dve", "act", "gp", "cc")
        + tuple(f"dx{i}" for i in range(NDMASEM))
        + tuple(f"dw{i}" for i in range(NDMASEM))
        + tuple(f"ds{i}" for i in range(NDSSEM)))


def build_nc(debug=False):
    nc = bass.Bass("TRN2", target_bir_lowering=False)
    d = nc.dram_tensor

    i_x = d("xin", [TPAD, HID], F32, kind="ExternalInput")
    i_wT = d("wT", [HID, 8480], F32, kind="ExternalInput")
    i_cdiag = d("cdiag", [NCT, 4, 128, 128], F32, kind="ExternalInput")
    i_convb = d("convb", [1, CONVD], F32, kind="ExternalInput")
    i_alog = d("alog", [HEADS, 1], F32, kind="ExternalInput")
    i_dtb = d("dtb", [128, HEADS], F32, kind="ExternalInput")
    i_dd = d("dd", [128, HEADS], F32, kind="ExternalInput")
    i_nw = d("nw", [128, INTER], F32, kind="ExternalInput")
    i_woT = d("woT", [INTER, HID], F32, kind="ExternalInput")
    i_wr = d("wr", [HID, NE], F32, kind="ExternalInput")
    i_wg = d("wg", [HID, KE * FI], F32, kind="ExternalInput")
    i_wu = d("wu", [HID, KE * FI], F32, kind="ExternalInput")
    i_wd = d("wd", [FI, HID], F32, kind="ExternalInput")
    i_swg = d("swg", [HID, H2], F32, kind="ExternalInput")
    i_swu = d("swu", [HID, H2], F32, kind="ExternalInput")
    i_swd = d("swd", [H2, HID], F32, kind="ExternalInput")
    i_amp = d("amp", [NE, 16], F32, kind="ExternalInput")
    i_esc = d("esc", [NE, 1], F32, kind="ExternalInput")
    i_idmb = d("idmb", [128, 128], BF16, kind="ExternalInput")
    i_idmf = d("idmf", [128, 128], F32, kind="ExternalInput")
    i_dcapa = d("dcapa", [128, 256], F32, kind="ExternalInput")
    i_dcapb = d("dcapb", [128, 128], F32, kind="ExternalInput")
    i_ones = d("onesr", [1, 128], F32, kind="ExternalInput")
    i_selb = d("selb", [HEADS, HEADS * 128], F32, kind="ExternalInput")
    i_mask = d("maskv", [128, 8], F32, kind="ExternalInput")
    o_y = d("yout", [TLOC, HID], F32, kind="ExternalOutput")

    kd = "ExternalOutput" if debug else "Internal"
    h_xln = d("h_xln", [TPAD, HID], BF16, kind=kd)
    h_sg = d("h_sg", [TLOC, INTER], BF16, kind=kd)
    h_ytot = d("h_ytot", [TLOC, INTER], BF16, kind=kd)
    h_y3 = d("h_y3", [TLOC, INTER], BF16, kind=kd)
    h_x1 = d("h_x1", [TLOC, HID], F32, kind=kd)
    h_xln2 = d("h_xln2", [TLOC, HID], BF16, kind=kd)
    h_m = d("h_m", [TLOC, FI], BF16, kind=kd)
    h_smid = d("h_smid", [TLOC, H2], BF16, kind=kd)
    h_sout = d("h_sout", [128, HEADS * 128], BF16)
    h_sgat = d("h_sgat", [8 * 128, HEADS * 128], BF16)
    if debug:
        t_convo = d("t_convo", [TLOC, CONVD], BF16, kind="ExternalOutput")
        t_dt = d("t_dt", [128, TT * HEADS], F32, kind="ExternalOutput")
        t_ac = d("t_ac", [32, TLOC], F32, kind="ExternalOutput")
        t_ao = d("t_ao", [TLOC, HID], F32, kind="ExternalOutput")
        t_ffo = d("t_ffo", [TLOC, HID], F32, kind="ExternalOutput")
        t_comb = d("t_comb", [128, TT * KE], F32, kind="ExternalOutput")
        t_probs = d("t_probs", [TLOC, NE], F32, kind="ExternalOutput")
        t_sout = d("t_sout", [128, HEADS * 128], BF16, kind="ExternalOutput")
        t_cols = d("t_cols", [128, TT * 6], F32, kind="ExternalOutput")
        t_xr = d("t_xr", [128, 2 * HID], F32, kind="ExternalOutput")
        t_x1t = d("t_x1t", [128, 2 * HID], F32, kind="ExternalOutput")

    ARENA = 204 * 1024
    arena = nc.alloc_sbuf_tensor("arena", [128, ARENA], mybir.dt.uint8)
    base = nc.lookup_mloc(arena).addr

    def at(name, shape, dtype, off):
        sz = int(np.prod(shape[1:])) * mybir.dt.size(dtype)
        assert off + sz <= ARENA, f"{name} overflows arena: {off}+{sz}"
        return nc.alloc_sbuf_tensor_at(name, shape, dtype, offset=base + off)

    A, B, C, D = 0, 73728, 139264, 176128
    x_sb = at("x_sb", [128, 9, HID], F32, A)
    convo = at("convo", [128, TT, CONVD], BF16, A)
    ysb = at("ysb", [128, 2, 2, INTER], BF16, A + 0)
    y2 = at("y2", [128, 2, INTER], BF16, A + 32768)
    y3t = at("y3t", [128, 2, INTER], BF16, A + 49152)
    junkb = at("junkb", [128, INTER], BF16, A + 65536)
    argm = at("argm", [128, 384], F32, A + 69632)
    e_sb = at("e_sb", [128, 384], BF16, A + 71168)
    mt_sb = at("mt_sb", [128, 384], BF16, A + 71936)
    erow = at("erow", [128, 256], BF16, A + 72704)
    ceb = at("ceb", [128, 256], BF16, A + 73216)
    yT = at("yT", [128, 32, TLOC], BF16, A)
    xr = at("xr", [128, 2, HID], F32, A + 0)
    x1t = at("x1t", [128, 2, HID], F32, A + 16384)
    xl2t = at("xl2t", [128, 2, HID], F32, A + 32768)
    wga = at("wga", [128, KH, 512], BF16, A + 0)
    wua = at("wua", [128, KH, 512], BF16, A + 16384)
    wgb = at("wgb", [128, KH, 512], BF16, A + 32768)
    wub = at("wub", [128, KH, 512], BF16, A + 49152)
    sg1 = at("sg1", [128, 2, 512], BF16, A + 65536)
    bas1 = at("bas1", [128, 2, 512], BF16, A + 67584)
    ffo = at("ffo", [128, TT, HID], F32, A)
    xln_sb = at("xln_sb", [128, 9, HID], BF16, B + 0)
    zrow = at("zrow", [128, HID], BF16, B + 36864)
    hbc = at("hbc", [128, 3, 1032], BF16, B + 0)
    convb_bf = at("convb_bf", [1, CONVD], BF16, B + 40960)
    gwa = at("gwa", [128, KH, 256], BF16, B + 40960)
    S = at("S", [128, HEADS * 128], F32, B + 0)
    S_bf = at("S_bf", [128, HEADS * 128], BF16, B + 16384)
    ytot = at("ytot", [128, 2, INTER], BF16, B + 24576)
    dt_fm = at("dt_fm", [32, TLOC], F32, B + 40960)
    ad_fm = at("ad_fm", [32, TLOC], F32, B + 45056)
    ac_fm = at("ac_fm", [32, TLOC], F32, B + 49152)
    ac_hi = at("ac_hi", [32, TLOC], BF16, B + 53248)
    ac_lo = at("ac_lo", [32, TLOC], BF16, B + 55296)
    dcy_fm = at("dcy_fm", [32, TLOC], F32, B + 57344)
    zeros32 = at("zeros32", [32, CH], F32, B + 61440)
    srow = at("srow", [1, HEADS], F32, B + 62464)
    b_fm = at("b_fm", [128, TT, 128], BF16, B + 61440)
    c_fm = at("c_fm", [128, TT, 128], BF16, B + 63488)
    S_in = at("S_in", [128, HEADS * 128], BF16, B + 0)
    nw_bf = at("nw_bf", [128, INTER], BF16, B + 8192)
    sgat8 = at("sgat8", [128, 2, HEADS * 128], BF16, B + 16384)
    sgc = at("sgc", [128, 2, 2, INTER], BF16, B + 32768)
    ao = at("ao", [128, TT, HID], F32, B)
    probs = at("probs", [128, TT, NE], F32, B + 0)
    ctm = at("ctm", [128, TT, NE], BF16, B + 4096)
    cfm = at("cfm", [128, TT, NE], BF16, B + 6144)
    comb = at("comb", [128, TT, KE], F32, B + 8192)
    sq16 = at("sq16", [128, 16], F32, B + 8512)
    p8 = at("p8", [128, KE], F32, B + 8576)
    pcol = at("pcol", [128, 1], F32, B + 8640)
    m_sb = at("m_sb", [128, TT, FI], F32, B + 12288)
    smid = at("smid", [128, TT, H2], BF16, B + 0)
    smidT = at("smidT", [128, KH, TLOC], BF16, B + 32768)
    swda = at("swda", [128, KH, 512], BF16, B + 0)
    swdb = at("swdb", [128, KH, 512], BF16, B + 16384)
    x1r = at("x1r", [128, 2, HID], F32, B + 32768)
    junk10 = at("junk10", [128, HID], F32, B + 0)
    xlnT = at("xlnT", [128, KH, TPAD], BF16, C)
    c_fm2 = at("c_fm2", [128, TT, 128], BF16, C + 0)
    ysc2 = at("ysc2", [128, 2, 256], F32, C + 7168)
    ecq = at("ecq", [128, 2, 2, HEADS], F32, C + 9216)
    q_bc = at("q_bc", [128, NCH, HEADS], F32, C + 13824)
    wo_a = at("wo_a", [128, 8, 512], BF16, C + 0)
    wo_b = at("wo_b", [128, 8, 512], BF16, C + 8192)
    xlnT2 = at("xlnT2", [128, KH, TLOC], BF16, C + 0)
    mT = at("mT", [128, 8, TLOC], BF16, C + 0)
    wdb_a = at("wdb_a", [128, 8, 512], BF16, C + 16384)
    wdb_b = at("wdb_b", [128, 8, 512], BF16, C + 24576)
    x2t = at("x2t", [128, 2, HID], F32, C + 16384)
    doff = [D]

    def atD(name, shape, dtype):
        t = at(name, shape, dtype, doff[0])
        sz = int(np.prod(shape[1:])) * mybir.dt.size(dtype)
        doff[0] += (sz + 31) // 32 * 32
        return t

    idmb = atD("idmbS", [128, 128], BF16)
    idmf = atD("idmfS", [128, 128], F32)
    dcapa = atD("dcapaS", [128, 256], F32)
    dcapb = atD("dcapbS", [128, 128], F32)
    ones_f = atD("ones_fS", [1, 128], F32)
    ones_b = atD("ones_bS", [1, 128], BF16)
    selb = atD("selbS", [HEADS, HEADS * 128], BF16)
    dtb = atD("dtbS", [128, HEADS], F32)
    ddv = atD("ddvS", [128, HEADS], F32)
    alog = atD("alogS", [HEADS, 1], F32)
    a_e = atD("a_eS", [HEADS, 1], F32)
    esc = atD("escS", [NE, 1], F32)
    amp = atD("ampS", [NE, 16], F32)
    maskv = atD("maskvS", [128, 8], F32)
    mixs = atD("mixsS", [NE, KE], BF16)
    stat = atD("statS", [128, 24], F32)
    stat2 = atD("stat2S", [128, 2], F32)
    rcol = atD("rcolS", [128, 1], F32)
    bcol = atD("bcolS", [128, 1], F32)
    ucol = atD("ucolS", [128, 1], F32)
    s4c = atD("s4cS", [128, 1], F32)
    w8 = atD("w8S", [128, 8], F32)
    mskt = atD("msktS", [128, NE], F32)
    mcol = atD("mcolS", [128, 1], F32)
    scol = atD("scolS", [128, 1], F32)
    vcol = atD("vcolS", [128, 1], F32)
    m2col = atD("m2colS", [128, 1], F32)
    s2col = atD("s2colS", [128, 1], F32)
    v2col = atD("v2colS", [128, 1], F32)
    u2col = atD("u2colS", [128, 1], F32)
    r2col = atD("r2colS", [128, 1], F32)
    b2col = atD("b2colS", [128, 1], F32)
    ysq = atD("ysqS", [128, 1], F32)
    rtc = atD("rtcS", [128, 1], F32)
    rr2 = atD("rr2S", [128, 1], F32)
    dtraw = atD("dtrawS", [128, TT, HEADS], F32)
    dtv = atD("dtvS", [128, TT, HEADS], F32)
    sp1 = atD("sp1S", [128, TT, HEADS], F32)
    acT = atD("acTS", [128, TT, HEADS], F32)
    eaT = atD("eaTS", [128, TT, HEADS], F32)
    dT = atD("dTS", [128, TT, HEADS], F32)
    r_bc = atD("r_bcS", [128, NCH, HEADS], F32)
    wip_off = doff[0]
    wip = atD("wipS", [128, 2, KH, 128], BF16)
    cdt_off = doff[0]
    cdt = atD("cdtS", [128, 2, 4, 128], BF16)
    wdt_off = doff[0]
    wdt = atD("wdtS", [128, KH, HEADS], BF16)
    gwb = at("gwb", [128, KH, 256], BF16, wip_off)
    sgblk = at("sgblk2", [128, 2, 256], BF16, cdt_off)
    xdt = at("xdt", [128, 2, 2, 128], BF16, cdt_off + 1024)
    bdb = at("bdb", [128, 2, 2, 128], BF16, wdt_off)
    wrb = at("wrbS", [128, KH, NE], BF16, C + 32768)
    dcols = atD("dcolsS", [128, TT, 6], F32)
    assert doff[0] <= ARENA, f"D overflow {doff[0]}"

    pb = [nc.alloc_psum_tensor(f"pb{i}", [128, 512], F32) for i in range(6)]
    pT = nc.alloc_psum_tensor("pT", [128, 1024], BF16)
    pF = nc.alloc_psum_tensor("pF", [128, 128], F32)

    sh = {s: nc.semaphore(f"s_{s}").__enter__() for s in SEMS}

    ops = []
    cnt = {s: 0 for s in SEMS}
    marks = {}

    def OP(eng, fn, inc=None, waits=()):
        ops.append((eng, fn, inc, tuple(waits)))
        if inc is not None:
            cnt[inc[0]] += inc[1]

    def MK(name, sem):
        marks[name] = [(sem, cnt[sem])]

    def MKQ(name, q):
        p = "dx" if q == "sync" else "dw"
        marks[name] = [(f"{p}{i}", cnt[f"{p}{i}"]) for i in range(NDMASEM)
                       if cnt[f"{p}{i}"] > 0]

    dma_idx = {"sync": 0, "gp": 0, "act": 0}

    def dma(q, out, in_, waits=(), mark=None, **kw):
        pfx = {"sync": "dx", "gp": "dw", "act": "ds"}[q]
        n = NDSSEM if q == "act" else NDMASEM
        s = f"{pfx}{dma_idx[q] % n}"
        dma_idx[q] += 1
        OP(q, lambda e, out=out, in_=in_, kw=kw: e.dma_start(out=out, in_=in_, **kw),
           inc=(s, 16), waits=waits)
        if mark:
            MK(mark, s)
        return s

    def mmg(lst, waits=(), mark=None):
        n = len(lst)
        for q, (o, l, r) in enumerate(lst):
            OP("pe",
               lambda e, o=o, l=l, r=r, st=(q == 0), sp=(q == n - 1):
               e.matmul(o, l, r, start=st, stop=sp),
               inc=("pe", 1) if q == n - 1 else None,
               waits=waits if q == 0 else ())
        if mark:
            MK(mark, "pe")

    def tpose(out_ps, in_ap, ident, waits=(), mark=None):
        OP("pe", lambda e, o=out_ps, i=in_ap, dd=ident: e.transpose(o, i, dd),
           inc=("pe", 1), waits=waits)
        if mark:
            MK(mark, "pe")

    def dve(fn, waits=(), mark=None):
        OP("dve", fn, inc=("dve", 1), waits=waits)
        if mark:
            MK(mark, "dve")

    def act(fn, waits=(), mark=None):
        OP("act", fn, inc=("act", 1), waits=waits)
        if mark:
            MK(mark, "act")

    COLS = [(mcol, scol, vcol, ucol, rcol, bcol),
            (m2col, s2col, v2col, u2col, r2col, b2col)]

    def ln_stats(src_ap, junk_ap, dwaits, awaits, pfx, cs=0):
        mc, sc, vc, uc, rc, bc = COLS[cs]
        dve(lambda e, s=src_ap: e.reduce_sum(mc[:, :], s, axis=AX.X),
            waits=dwaits, mark=f"{pfx}m")
        act(lambda e, s=src_ap, jk=junk_ap: e.activation(jk, s, AF.Square,
                                                         accum_out=sc[:, :]),
            waits=awaits, mark=f"{pfx}s")
        dve(lambda e: e.tensor_scalar(vc[:, :], sc[:, :], 1.0 / HID, None, ALU.mult),
            waits=[("act", f"{pfx}s")])
        dve(lambda e: e.tensor_scalar(mc[:, :], mc[:, :], 1.0 / HID, None, ALU.mult),
            mark=f"{pfx}mm")
        dve(lambda e: e.scalar_tensor_tensor(uc[:, :], mc[:, :], mc[:, :],
                                             vc[:, :], ALU.mult, ALU.subtract),
            waits=[("dve", f"{pfx}mm")], mark=f"{pfx}u")
        dve(lambda e: e.tensor_scalar(uc[:, :], uc[:, :], -1.0, EPS_LN,
                                      ALU.mult, ALU.add),
            waits=[("dve", f"{pfx}u")], mark=f"{pfx}v")
        act(lambda e: e.activation(bc[:, :], uc[:, :], AF.Ln),
            waits=[("dve", f"{pfx}v")])
        act(lambda e: e.activation(rc[:, :], bc[:, :], AF.Exp, scale=-0.5),
            mark=f"{pfx}r")
        dve(lambda e: e.tensor_scalar(bc[:, :], mc[:, :], rc[:, :], -1.0,
                                      ALU.mult, ALU.mult),
            waits=[("act", f"{pfx}r")], mark=f"{pfx}b")

    dma("sync", x_sb[:, :, :], i_x.ap().rearrange("(i p) d -> p i d", p=128), mark="x_ld")
    for t, src in [(idmb, i_idmb), (idmf, i_idmf), (dcapa, i_dcapa), (dcapb, i_dcapb),
                   (ones_f, i_ones), (alog, i_alog), (dtb, i_dtb), (ddv, i_dd),
                   (esc, i_esc), (amp, i_amp), (maskv, i_mask)]:
        dma("sync", t[tuple(slice(None) for _ in t.shape)], src.ap())
    MKQ("consts_ld", "sync")
    dma("gp", selb[:, :], i_selb.ap())
    dma("gp", ones_b[:, :], i_ones.ap())
    dma("gp", wdt[:, :, :], i_wT.ap()[:, 8448:8480].rearrange("(k p) c -> p k c", p=128),
        mark=None)
    MKQ("smallw_ld", "gp")

    dve(lambda e: e.memset(zrow[:, :], 0.0), waits=[("dx", "x_ld")], mark="zrow")
    for i in range(9):
        fw = [("dx", "x_ld"), ("dx", "consts_ld")]
        fwa = list(fw) + ([("act", f"ln1x_{i - 2}")] if i > 1 else [])
        ln_stats(x_sb[:, i, :], xln_sb[:, i, :],
                 fw + ([("act", f"ln1x_{i - 2}")] if i > 1 else []), fwa,
                 f"ln1_{i}", cs=i % 2)
        rc1, bc1 = COLS[i % 2][4], COLS[i % 2][5]
        act(lambda e, i=i, rc1=rc1, bc1=bc1:
            e.activation(xln_sb[:, i, :], x_sb[:, i, :], AF.Identity,
                         bias=bc1[:, :], scale=rc1[:, :]),
            waits=[("dve", f"ln1_{i}b")], mark=f"ln1x_{i}")
    MK("xln_done", "act")
    dma("sync", h_xln.ap()[3:3 + 512, :].rearrange("(i p) d -> p i d", p=128),
        xln_sb[:, 0:4, :], waits=[("act", "ln1x_3")])
    dma("sync", h_xln.ap()[0:3, :], xln_sb[0:3, 8, :], waits=[("act", "xln_done")])
    MKQ("xlnhA", "sync")
    dma("sync", h_xln.ap()[515:515 + 512, :].rearrange("(i p) d -> p i d", p=128),
        xln_sb[:, 4:8, :], waits=[("act", "ln1x_7")])
    dma("sync", h_xln.ap()[1027:TPAD, :], zrow[0:125, :], waits=[("dve", "zrow")],
        mark=None)
    MKQ("xlnh", "sync")
    for kk in range(KH):
        dma("sync", xlnT[:, kk, 0:512], h_xln.ap()[0:512, 128 * kk:128 * (kk + 1)],
            transpose=True, waits=[("dx", "xlnhA")])
    for kk in range(KH):
        dma("sync", xlnT[:, kk, 512:TPAD], h_xln.ap()[512:TPAD, 128 * kk:128 * (kk + 1)],
            transpose=True, waits=[("dx", "xlnh")])
    MKQ("xlnT", "sync")
    dma("gp", convb_bf[:, :], i_convb.ap(), waits=[("dx", "xlnh")], mark="convb_ld")

    NBS = [(0, 512), (512, 512), (1024, 8)]
    for m in range(NCT):
        wts = [("pe", f"ipu_{m - 2}")] if m >= 2 else [("dx", "xlnh")]
        dma("gp", wip[:, m % 2, :, :],
            i_wT.ap()[:, GATE + 128 * m:GATE + 128 * (m + 1)].rearrange("(k p) c -> p k c", p=128),
            waits=wts, mark=f"wip_{m}")
        cwts = [("pe", f"cvu_{m - 2}")] if m >= 2 else []
        dma("gp", cdt[:, m % 2, :, :], i_cdiag.ap()[m, :, :, :].rearrange("k p c -> p k c"),
            waits=cwts, mark=f"cdt_{m}")
        for nb, (o0, nn) in enumerate(NBS):
            g = m * 3 + nb
            w = [("dw", f"wip_{m}"), ("dx", "xlnT")]
            if g >= 2:
                gm, gn = divmod(g - 2, 3)
                w.append(("dve", f"ipe_{gm}_{gn}"))
            bank = pb[g % 2]
            mmg([(bank[:, 0:nn], wip[:, m % 2, k, :], xlnT[:, k, o0:o0 + nn])
                 for k in range(KH)], waits=w, mark=f"ipm_{m}_{nb}")
            dve(lambda e, bank=bank, m=m, o0=o0, nn=nn:
                e.tensor_copy(hbc[:, m % 3, o0:o0 + nn], bank[:, 0:nn]),
                waits=[("pe", f"ipm_{m}_{nb}")], mark=f"ipe_{m}_{nb}")
        MK(f"ipu_{m}", "pe")
        for j in range(TT):
            half = j // 4
            bank = pb[2 + half]
            reg = bank[:, 128 * (j % 4):128 * (j % 4) + 128]
            w = [("dve", f"ipe_{m}_2"), ("dw", f"cdt_{m}"), ("dw", "smallw_ld"),
                 ("dw", "convb_ld")]
            if m >= 1 and j % 4 == 0:
                w.append(("act", f"cve_{m - 1}_{half}"))
            lst = [(reg, hbc[:, m % 3, 128 * j + k:128 * j + k + 128], cdt[:, m % 2, k, :])
                   for k in range(4)]
            lst.append((reg, ones_b[:, :], convb_bf[0:1, 128 * m:128 * (m + 1)]))
            mmg(lst, waits=w, mark=f"cvm_{m}_{j}")
        MK(f"cvu_{m}", "pe")
        for half in range(2):
            act(lambda e, m=m, half=half:
                e.activation(convo[:, 4 * half:4 * half + 4, 128 * m:128 * (m + 1)],
                             pb[2 + half][:, 0:512], AF.Silu),
                waits=[("pe", f"cvm_{m}_{4 * half + 3}")], mark=f"cve_{m}_{half}")
    MK("conv_done", "act")
    if debug:
        dma("sync", t_convo.ap().rearrange("(i p) d -> p i d", p=128), convo[:, :, :],
            waits=[("act", "conv_done")])

    for j in range(TT):
        reg = pb[4][:, 32 * j:32 * (j + 1)]
        mmg([(reg, xlnT[:, k, 3 + 128 * j:3 + 128 * j + 128], wdt[:, k, :])
             for k in range(KH)],
            waits=[("dx", "xlnT"), ("dw", "smallw_ld")], mark=f"dtm_{j}")
    dve(lambda e: e.tensor_copy(dtraw[:, :, :],
                                pb[4][:, 0:256].rearrange("p (a b) -> p a b", a=TT)),
        waits=[("pe", "dtm_7")], mark="dtraw_ev")

    dtb_bc = bass.AP(dtb, 0, [[HEADS, 128], [0, TT], [1, HEADS]])
    dve(lambda e: e.tensor_tensor(dtv[:, :, :], dtraw[:, :, :], dtb_bc, ALU.add),
        waits=[("dve", "dtraw_ev"), ("dx", "consts_ld")], mark="dtz")
    act(lambda e: e.activation(sp1[:, :, :], dtv[:, :, :], AF.Abs),
        waits=[("dve", "dtz")], mark="sp_abs")
    act(lambda e: e.activation(dtraw[:, :, :], sp1[:, :, :], AF.Exp, scale=-1.0),
        mark="sp_exp")
    act(lambda e: e.activation(eaT[:, :, :], dtraw[:, :, :], AF.Ln, bias=1.0),
        mark="sp_ln")
    dve(lambda e: e.tensor_scalar(sp1[:, :, :], dtv[:, :, :], 0.0, None, ALU.max),
        waits=[("act", "sp_exp")], mark="spm")
    dve(lambda e: e.tensor_tensor(dtv[:, :, :], sp1[:, :, :], eaT[:, :, :], ALU.add),
        waits=[("act", "sp_ln"), ("dve", "spm")], mark="spa")
    dve(lambda e: e.tensor_scalar(dtv[:, :, :], dtv[:, :, :], 100.0, None, ALU.min),
        waits=[("dve", "spa")], mark="dt_done")
    for j in range(TT):
        w = [("dve", "dt_done")]
        if j > 0:
            w.append(("dve", f"dtfm_{j - 1}"))
        tpose(pF[0:32, 0:128], dtv[:, j, :], idmf[:, :], waits=w, mark=f"dtT_{j}")
        dve(lambda e, j=j: e.tensor_copy(dt_fm[:, 128 * j:128 * (j + 1)], pF[0:32, 0:128]),
            waits=[("pe", f"dtT_{j}"), ("dw", "convb_ld")], mark=f"dtfm_{j}")
    act(lambda e: e.activation(a_e[:, :], alog[:, :], AF.Exp),
        waits=[("dx", "consts_ld")], mark="a_e")
    dve(lambda e: e.tensor_scalar(ad_fm[:, :], dt_fm[:, :], a_e[:, :], -1.0,
                                  ALU.mult, ALU.mult),
        waits=[("dve", "dtfm_7"), ("act", "a_e")], mark="ad")
    dve(lambda e: e.memset(zeros32[:, :], 0.0), mark="z32")
    for c in range(NCH):
        dve(lambda e, c=c: e.tensor_tensor_scan(
            ac_fm[:, CH * c:CH * (c + 1)], ad_fm[:, CH * c:CH * (c + 1)],
            zeros32[:, :], 0.0, ALU.add, ALU.add),
            waits=[("dve", "ad"), ("dve", "z32")] if c == 0 else ())
    MK("acum", "dve")
    dve(lambda e: e.tensor_copy(ac_hi[:, :], ac_fm[:, :]),
        waits=[("dve", "acum")], mark="achi")
    dve(lambda e: e.tensor_tensor(ac_lo[:, :], ac_fm[:, :], ac_hi[:, :], ALU.subtract),
        waits=[("dve", "achi")], mark="achl")
    for c in range(NCH):
        dve(lambda e, c=c: e.tensor_scalar(
            ad_fm[:, CH * c:CH * (c + 1)], ac_fm[:, CH * c:CH * (c + 1)],
            ac_fm[:, CH * c + 255:CH * c + 256], -1.0, ALU.subtract, ALU.mult),
            waits=[("dve", "achl")])
    MK("dcy_pre", "dve")
    act(lambda e: e.activation(dcy_fm[:, :], ad_fm[:, :], AF.Exp),
        waits=[("dve", "dcy_pre")], mark="dcy")
    for j in range(TT):
        w = [("dve", "acum"), ("dve", f"dtfm_{TT - 1}")]
        if j > 0:
            w.append(("dve", f"acT_{j - 1}"))
        tpose(pF[0:128, 0:32], ac_fm[:, 128 * j:128 * (j + 1)], idmf[0:32, 0:32],
              waits=w, mark=f"acTp_{j}")
        dve(lambda e, j=j: e.tensor_copy(acT[:, j, :], pF[0:128, 0:32]),
            waits=[("pe", f"acTp_{j}")], mark=f"acT_{j}")
    for j in range(TT):
        w = [("act", "dcy"), ("dve", "acT_7" if j == 0 else f"dT_{j - 1}")]
        tpose(pF[0:128, 0:32], dcy_fm[:, 128 * j:128 * (j + 1)], idmf[0:32, 0:32],
              waits=w, mark=f"dTp_{j}")
        dve(lambda e, j=j: e.tensor_copy(dT[:, j, :], pF[0:128, 0:32]),
            waits=[("pe", f"dTp_{j}")], mark=f"dT_{j}")
    act(lambda e: e.activation(eaT[:, :, :], acT[:, :, :], AF.Exp),
        waits=[("dve", "acT_7")], mark="eaT")
    for c in range(NCH):
        w = [("dve", "dT_7" if c == 0 else f"srow_{c - 1}")]
        if c > 0:
            w.append(("act", f"rbc_{c - 1}"))
        tpose(pF[0:1, 0:32], ac_fm[:, CH * c + 255:CH * c + 256], idmf[0:32, 0:32],
              waits=w, mark=f"srT_{c}")
        dve(lambda e: e.tensor_copy(srow[:, :], pF[0:1, 0:32]),
            waits=[("pe", f"srT_{c}")], mark=f"srow_{c}")
        mmg([(pF[0:128, 0:32], ones_f[:, :], srow[:, :])],
            waits=[("dve", f"srow_{c}")], mark=f"rbm_{c}")
        act(lambda e, c=c: e.activation(r_bc[:, c, :], pF[0:128, 0:32], AF.Exp),
            waits=[("pe", f"rbm_{c}")], mark=f"rbc_{c}")
    MK("scanprep", "act")

    if debug:
        dma("sync", t_dt.ap(), dtv[:, :, :].rearrange("p a b -> p (a b)"),
            waits=[("dve", "dt_done")])
        dma("sync", t_ac.ap(), ac_fm[:, :], waits=[("dve", "achl")])
    for j in range(TT):
        w = [("act", "conv_done")]
        if j > 0:
            w.append(("dve", f"bfm_{j - 1}"))
        tpose(pT[:, 0:128], convo[:, j, 4096:4224], idmb[:, :], waits=w, mark=f"bTp_{j}")
        dve(lambda e, j=j: e.tensor_copy(b_fm[:, j, :], pT[:, 0:128]),
            waits=[("pe", f"bTp_{j}")], mark=f"bfm_{j}")
    for j in range(TT):
        w = [("dve", "bfm_7" if j == 0 else f"cfm_{j - 1}")]
        tpose(pT[:, 0:128], convo[:, j, 4224:4352], idmb[:, :], waits=w, mark=f"cTp_{j}")
        dve(lambda e, j=j: e.tensor_copy(c_fm[:, j, :], pT[:, 0:128]),
            waits=[("pe", f"cTp_{j}")], mark=f"cfm_{j}")
    dve(lambda e: e.memset(S[:, :], 0.0))
    dve(lambda e: e.memset(S_bf[:, :], 0.0), mark="S0")

    def evac_head(c, h):
        j0 = 2 * c
        Yb = pb[3 + h % 2]
        for jj in (0, 1):
            w = [("pe", f"y_{c}_{h}")] if jj == 0 else []
            if jj == 0 and h == 0:
                w += [("dx", f"ytsp_{c - 1}")] if c > 0 else [("dx", "xlnT")]
            dve(lambda e, Yb=Yb, jj=jj, j0=j0, h=h:
                e.scalar_tensor_tensor(ytot[:, jj, 128 * h:128 * (h + 1)],
                                       convo[:, j0 + jj, 128 * h:128 * (h + 1)],
                                       ddv[:, h:h + 1], Yb[:, 128 * jj:128 * (jj + 1)],
                                       ALU.mult, ALU.add),
                waits=w)
        MK(f"yt_{c}_{h}", "dve")
        dve(lambda e, c=c, h=h:
            e.scalar_tensor_tensor(S[:, 128 * h:128 * (h + 1)],
                                   S[:, 128 * h:128 * (h + 1)], r_bc[:, c, h:h + 1],
                                   pF[:, 0:128], ALU.mult, ALU.add),
            waits=[("pe", f"st_{c}_{h}"), ("act", "scanprep")], mark=f"supd_{c}_{h}")
        dve(lambda e, h=h: e.tensor_copy(S_bf[:, 128 * h:128 * (h + 1)],
                                         S[:, 128 * h:128 * (h + 1)]),
            waits=[("pe", f"y_{c}_{h}"), ("dve", f"supd_{c}_{h}")], mark=f"sbf_{c}_{h}")

    for c in range(NCH):
        j0, j1 = 2 * c, 2 * c + 1
        for q in range(4):
            nh = 4 * c + q
            gw = gwa if nh % 2 == 0 else gwb
            if nh >= 2:
                w = [("pe", f"gau_{nh - 2}")]
            elif nh == 0:
                w = [("dve", "ad"), ("act", "dcy")]
            else:
                w = [("pe", f"cvu_{NCT - 1}")]
            dma("gp", gw[:, :, :],
                i_wT.ap()[:, 256 * nh:256 * (nh + 1)].rearrange("(k p) c -> p k c", p=128),
                waits=w, mark=f"gaw_{nh}")
        crhs = c_fm[:, j0:j0 + 2, :].rearrange("p a b -> p (a b)")
        w = [("dve", "cfm_7"), ("dve", "S0")]
        if c > 0:
            w.append(("dve", f"mt_{c - 1}_31"))
        mmg([(pb[0][:, 0:256], b_fm[:, j0, :], crhs)], waits=w)
        mmg([(pb[0][:, 256:512], b_fm[:, j1, :], crhs)], mark=f"gt_{c}")

        def emit_R(h, c=c):
            bank = pb[1 + h % 2]
            w = [("dve", "achl"), ("dw", "smallw_ld")]
            if h >= 2:
                w += [("dve", f"argm_{c}_{h - 2}"), ("act", f"er_{c}_{h - 2}")]
            elif c > 0:
                w += [("dve", f"argm_{c - 1}_{30 + h}"), ("act", f"er_{c - 1}_{30 + h}")]
            mmg([(bank[:, 0:256], selb[:, 128 * h:128 * (h + 1)],
                  ac_hi[:, CH * c:CH * (c + 1)]),
                 (bank[:, 0:256], selb[:, 128 * h:128 * (h + 1)],
                  ac_lo[:, CH * c:CH * (c + 1)])],
                waits=w, mark=f"R_{c}_{h}")

        emit_R(0)
        emit_R(1)
        for h in range(HEADS):
            pp = h % 2
            Rb = pb[1 + h % 2]
            g = 32 * c + h
            if h > 0:
                evac_head(c, h - 1)
            w = [("pe", f"R_{c}_{h}")]
            if h > 0:
                w.append(("act", f"E_{c}_{h - 1}"))
            elif c > 0:
                w.append(("act", f"E_{c - 1}_31"))
            dve(lambda e, Rb=Rb, j0=j0, h=h:
                e.scalar_tensor_tensor(argm[:, 0:256], Rb[:, 0:256],
                                       acT[:, j0, h:h + 1], dcapa[:, :],
                                       ALU.subtract, ALU.min),
                waits=w)
            dve(lambda e, Rb=Rb, j1=j1, h=h:
                e.scalar_tensor_tensor(argm[:, 256:384], Rb[:, 128:256],
                                       acT[:, j1, h:h + 1], dcapb[:, :],
                                       ALU.subtract, ALU.min),
                mark=f"argm_{c}_{h}")
            w = [("act", f"E_{c}_{h}")]
            if h > 0:
                w.append(("pe", f"y_{c}_{h - 1}"))
            elif c > 0:
                w.append(("pe", f"y_{c - 1}_31"))
            dve(lambda e: e.tensor_tensor(mt_sb[:, 0:256], e_sb[:, 0:256],
                                          pb[0][:, 0:256], ALU.mult),
                waits=w)
            dve(lambda e: e.tensor_tensor(mt_sb[:, 256:384], e_sb[:, 256:384],
                                          pb[0][:, 384:512], ALU.mult))
            dve(lambda e, j0=j0, c=c, h=h:
                e.tensor_tensor(ceb[:, :],
                                c_fm[:, j0:j0 + 2, :].rearrange("p a b -> p (a b)"),
                                erow[:, :], ALU.mult),
                waits=[("act", f"er_{c}_{h}")], mark=f"mt_{c}_{h}")
            w = [("dve", "dt_done"), ("dve", "dT_7")]
            if h >= 2:
                w.append(("pe", f"st_{c}_{h - 2}"))
            elif c > 0:
                w.append(("pe", f"st_{c - 1}_{30 + h}"))
            for jj in (0, 1):
                act(lambda e, pp=pp, jj=jj, j=j0 + jj, h=h:
                    e.activation(xdt[:, pp, jj, :], convo[:, j, 128 * h:128 * (h + 1)],
                                 AF.Copy, scale=dtv[:, j, h:h + 1]),
                    waits=w if jj == 0 else ())
            for jj in (0, 1):
                act(lambda e, pp=pp, jj=jj, j=j0 + jj, h=h:
                    e.activation(bdb[:, pp, jj, :], convo[:, j, 4096:4224],
                                 AF.Copy, scale=dT[:, j, h:h + 1]))
            MK(f"xb_{c}_{h}", "act")
            w = [("dve", f"argm_{c}_{h}")]
            if h > 0:
                w.append(("dve", f"mt_{c}_{h - 1}"))
            elif c > 0:
                w.append(("dve", f"mt_{c - 1}_31"))
            act(lambda e: e.activation(e_sb[:, 0:384], argm[:, 0:384], AF.Exp),
                waits=w, mark=f"E_{c}_{h}")
            act(lambda e, Rb=Rb: e.activation(erow[:, :], Rb[:, 0:256], AF.Exp),
                mark=f"er_{c}_{h}")
            if g > 0:
                gp, jp, np_ = g - 1, (g - 1) % 8, (g - 1) // 8
                w = [("pe", f"gam_{gp}")]
                if gp >= 2:
                    w.append(("dx", f"sgw_{gp - 2}"))
                act(lambda e, gp=gp: e.activation(sgblk[:, gp % 2, :], pb[5][:, 0:256],
                                                  AF.Copy),
                    waits=w, mark=f"sge_{gp}")
                dma("sync", h_sg.ap()[128 * jp:128 * (jp + 1),
                                      256 * np_:256 * (np_ + 1)],
                    sgblk[:, gp % 2, :], waits=[("act", f"sge_{gp}")], mark=f"sgw_{gp}")
            Yb = pb[3 + h % 2]
            w = [("dve", f"mt_{c}_{h}"), ("act", f"xb_{c}_{h}")]
            if h >= 2:
                w.append(("dve", f"yt_{c}_{h - 2}"))
            elif c > 0:
                w.append(("dve", f"yt_{c - 1}_{30 + h}"))
            else:
                w += [("act", "scanprep"), ("dve", f"ipe_{NCT - 1}_2"),
                      ("act", "conv_done"), ("dve", "dtraw_ev")]
            if c > 0:
                w.append(("dve", f"sbf_{c - 1}_{h}"))
            mmg([(Yb[:, 0:128], mt_sb[:, 0:128], xdt[:, pp, 0, :]),
                 (Yb[:, 0:128], ceb[:, 0:128], S_bf[:, 128 * h:128 * (h + 1)])],
                waits=w)
            mmg([(Yb[:, 128:256], mt_sb[:, 128:256], xdt[:, pp, 0, :]),
                 (Yb[:, 128:256], mt_sb[:, 256:384], xdt[:, pp, 1, :]),
                 (Yb[:, 128:256], ceb[:, 128:256], S_bf[:, 128 * h:128 * (h + 1)])],
                mark=f"y_{c}_{h}")
            w = []
            if h >= 1:
                w.append(("dve", f"supd_{c}_{h - 1}"))
            elif c > 0:
                w.append(("dve", f"supd_{c - 1}_31"))
            else:
                w.append(("act", "scanprep"))
            mmg([(pF[:, 0:128], bdb[:, pp, 0, :], xdt[:, pp, 0, :]),
                 (pF[:, 0:128], bdb[:, pp, 1, :], xdt[:, pp, 1, :])],
                waits=w, mark=f"st_{c}_{h}")
            nh, jg = g // 8, g % 8
            gw = gwa if nh % 2 == 0 else gwb
            w = [("dw", f"gaw_{nh}")]
            if g > 0:
                w.append(("act", f"sge_{g - 1}"))
            else:
                w += [("dve", f"ipe_{NCT - 1}_2"), ("act", "conv_done"),
                      ("dve", "dtraw_ev")]
            mmg([(pb[5][:, 0:256], xlnT[:, k, 3 + 128 * jg:3 + 128 * jg + 128],
                  gw[:, k, :]) for k in range(KH)], waits=w, mark=f"gam_{g}")
            if jg == 7:
                MK(f"gau_{nh}", "pe")
            if h + 2 < HEADS:
                emit_R(h + 2)
        evac_head(c, HEADS - 1)
        MK(f"chunk_{c}", "dve")
        dma("sync", h_ytot.ap()[CH * c:CH * (c + 1), :].rearrange("(i p) d -> p i d", p=128),
            ytot[:, :, :], waits=[("dve", f"chunk_{c}")], mark=f"ytsp_{c}")
    act(lambda e: e.activation(sgblk[:, 127 % 2, :], pb[5][:, 0:256], AF.Copy),
        waits=[("pe", "gam_127"), ("dx", "sgw_125")], mark="sge_127")
    dma("sync", h_sg.ap()[128 * 7:128 * 8, 256 * 15:256 * 16], sgblk[:, 1, :],
        waits=[("act", "sge_127")], mark="sgw_127")
    MKQ("sg_done", "sync")
    dve(lambda e: e.tensor_copy(c_fm2[:, :, :].rearrange("p a b -> p (a b)"),
                                c_fm[:, :, :].rearrange("p a b -> p (a b)")),
        waits=[("dve", "mt_3_31"), ("pe", "gau_15")], mark="cfm2")

    dve(lambda e: e.tensor_copy(S_bf[:, :], S[:, :]),
        waits=[("dve", "supd_3_31")], mark="sout_rdy")
    dma("gp", h_sout.ap(), S_bf[:, :], waits=[("dve", "sout_rdy")], mark="sout_sp")
    if debug:
        dma("sync", t_sout.ap(), S_bf[:, :], waits=[("dve", "sout_rdy")])
    OP("gp", lambda e: e.collective_compute(
        "AllGather", ALU.bypass, replica_groups=[list(range(8))],
        ins=[h_sout.ap().opt()], outs=[h_sgat.ap().opt()]),
       inc=("cc", 1), waits=[("dw", "sout_sp")])
    ccv = cnt["cc"]
    OP("gp", lambda e, v=ccv: e.wait_ge(sh["cc"], v), inc=None)
    for r in range(8):
        w = [("dve", f"sina_{r - 2}")] if r >= 2 else []
        dma("gp", sgat8[:, r % 2, :], h_sgat.ap()[128 * r:128 * (r + 1), :], waits=w,
            mark=f"sgat_{r}")
    dma("gp", nw_bf[:, :], i_nw.ap(), mark="nw_ld")

    for r in range(8):
        if r == 0:
            dve(lambda e: e.tensor_scalar(S_in[:, :], sgat8[:, 0, :], maskv[:, 0:1],
                                          None, ALU.mult),
                waits=[("dw", "sgat_0"), ("dw", "sout_sp"), ("dx", "consts_ld")],
                mark="sina_0")
        else:
            dve(lambda e, r=r: e.scalar_tensor_tensor(S_in[:, :], sgat8[:, r % 2, :],
                                                      maskv[:, r:r + 1], S_in[:, :],
                                                      ALU.mult, ALU.add),
                waits=[("dw", f"sgat_{r}"), ("dve", f"sina_{r - 1}")], mark=f"sina_{r}")
    MK("sin_done", "dve")
    dve(lambda e: e.memset(q_bc[:, 0, :], 1.0),
        waits=[("act", "scanprep"), ("pe", "gau_15")], mark="qb_0")
    for c in range(1, NCH):
        dve(lambda e, c=c: e.tensor_tensor(q_bc[:, c, :], q_bc[:, c - 1, :],
                                           r_bc[:, c - 1, :], ALU.mult),
            waits=[("dve", f"qb_{c - 1}")], mark=f"qb_{c}")
    MK("qbc", "dve")

    def corr_phase(c, pend=None):
        cpp = c % 2
        j0 = 2 * c
        w = [("dx", f"ytsp_{c}")]
        if c >= 2:
            w.append(("dve", f"ycons_{c - 2}"))
        dma("sync", ysb[:, cpp, :, :],
            h_ytot.ap()[CH * c:CH * (c + 1), :].rearrange("(i p) d -> p i d", p=128),
            waits=w, mark=f"yld_{c}")
        w = [("dx", "sg_done"), ("dve", "chunk_3"), ("dx", "ytsp_3"),
             ("pe", "gau_15"), ("dve", "cfm2")]
        if c >= 2:
            w.append(("dve", f"sgcons_{c - 2}"))
        dma("sync", sgc[:, cpp, :, :],
            h_sg.ap()[CH * c:CH * (c + 1), :].rearrange("(i p) d -> p i d", p=128),
            waits=w, mark=f"sgld_{c}")
        for jj in (0, 1):
            w = [("dve", "qbc"), ("act", "eaT")]
            if c >= 2 and jj == 0:
                w.append(("act", f"cse_{c - 2}_31"))
            dve(lambda e, c=c, jj=jj, j0=j0, cpp=cpp: e.tensor_tensor(
                ecq[:, cpp, jj, :], eaT[:, j0 + jj, :], q_bc[:, c, :], ALU.mult),
                waits=w)
        MK(f"ecq_{c}", "dve")
        for h in range(HEADS):
            bank = pb[h % 2]
            pp2 = h % 2
            w = [("dve", "sin_done"), ("dve", "cfm2")]
            if h >= 2:
                w.append(("act", f"cse_{c}_{h - 2}"))
            elif c > 0:
                w.append(("act", f"cse_{c - 1}_{30 + h}"))
            mmg([(bank[:, 0:128], c_fm2[:, j0, :], S_in[:, 128 * h:128 * (h + 1)])],
                waits=w)
            mmg([(bank[:, 128:256], c_fm2[:, j0 + 1, :], S_in[:, 128 * h:128 * (h + 1)])],
                mark=f"corr_{c}_{h}")
            for jj in (0, 1):
                act(lambda e, bank=bank, pp2=pp2, jj=jj, h=h:
                    e.activation(ysc2[:, pp2, 128 * jj:128 * (jj + 1)],
                                 bank[:, 128 * jj:128 * (jj + 1)], AF.Copy,
                                 scale=ecq[:, cpp, jj, h:h + 1]),
                    waits=([("pe", f"corr_{c}_{h}"), ("dve", f"ecq_{c}"),
                            ("pe", "gau_15")]
                           + ([("dve", f"cadd_{c}_{h - 1}")] if h > 0 else
                              ([("dve", f"cadd_{c - 1}_31")] if c > 0 else []))
                           if jj == 0 else ()))
            MK(f"cse_{c}_{h}", "act")
            for jj in (0, 1):
                dve(lambda e, cpp=cpp, pp2=pp2, jj=jj, h=h:
                    e.tensor_tensor(ysb[:, cpp, jj, 128 * h:128 * (h + 1)],
                                    ysb[:, cpp, jj, 128 * h:128 * (h + 1)],
                                    ysc2[:, pp2, 128 * jj:128 * (jj + 1)], ALU.add),
                    waits=([("act", f"cse_{c}_{h}"), ("dx", f"yld_{c}")]
                           if jj == 0 else ()))
            MK(f"cadd_{c}_{h}", "dve")
            if pend:
                pend.pop(0)()
                if h == HEADS - 1:
                    while pend:
                        pend.pop(0)()

    def gate_phase(c, defer=False):
        thunks = []

        def T(fn):
            if defer:
                thunks.append(fn)
            else:
                fn()
        cpp = c % 2
        for jj in (0, 1):
            w = [("dx", f"sgld_{c}")]
            if c > 0:
                w.append(("dve", f"y3_{c - 1}_{jj}"))
            T(lambda jj=jj, w=w: act(
                lambda e, cpp=cpp, jj=jj: e.activation(y2[:, jj, :],
                                                       sgc[:, cpp, jj, :], AF.Silu),
                waits=w, mark=f"silu_{c}_{jj}"))
        for jj in (0, 1):
            T(lambda jj=jj: dve(lambda e, cpp=cpp, jj=jj: e.tensor_tensor(
                y2[:, jj, :], y2[:, jj, :], ysb[:, cpp, jj, :], ALU.mult),
                waits=[("dve", f"cadd_{c}_31"), ("act", f"silu_{c}_{jj}"),
                       ("dx", f"yld_{c}")], mark=f"y2_{c}_{jj}"))
            T(lambda jj=jj: act(
                lambda e, jj=jj: e.activation(junkb[:, :], y2[:, jj, :], AF.Square,
                                              accum_out=ysq[:, :]),
                waits=[("dve", f"y2_{c}_{jj}")], mark=f"ysq_{c}_{jj}"))
            T(lambda jj=jj: dve(
                lambda e: e.tensor_scalar(rr2[:, :], ysq[:, :], 1.0 / INTER, EPS_RMS,
                                          ALU.mult, ALU.add),
                waits=[("act", f"ysq_{c}_{jj}")], mark=f"rmsv_{c}_{jj}"))
            T(lambda jj=jj: act(lambda e: e.activation(rr2[:, :], rr2[:, :], AF.Ln),
                                waits=[("dve", f"rmsv_{c}_{jj}")]))
            T(lambda jj=jj: act(lambda e: e.activation(rr2[:, :], rr2[:, :], AF.Exp,
                                                       scale=-0.5),
                                mark=f"rmsr_{c}_{jj}"))
            T(lambda jj=jj: dve(
                lambda e, jj=jj: e.tensor_scalar(y2[:, jj, :], y2[:, jj, :], rr2[:, :],
                                                 None, ALU.mult),
                waits=[("act", f"rmsr_{c}_{jj}")], mark=f"y2s_{c}_{jj}"))
            w = [("dw", "nw_ld")]
            if c > 0 or jj > 0:
                w.append(("dx", f"y3w_{c}_{jj - 1}" if jj else f"y3w_{c - 1}_1"))
            T(lambda jj=jj, w=w: dve(
                lambda e, jj=jj: e.tensor_tensor(y3t[:, jj, :], y2[:, jj, :],
                                                 nw_bf[:, :], ALU.mult),
                waits=w + [("dve", f"y2s_{c}_{jj}")], mark=f"y3_{c}_{jj}"))
            T(lambda jj=jj: dma(
                "sync", h_y3.ap()[CH * c + 128 * jj:CH * c + 128 * (jj + 1), :],
                y3t[:, jj, :], waits=[("dve", f"y3_{c}_{jj}")], mark=f"y3w_{c}_{jj}"))
        T(lambda: MK(f"ycons_{c}", "dve"))
        T(lambda: MK(f"sgcons_{c}", "dve"))
        return thunks

    corr_phase(0)
    corr_phase(1, pend=gate_phase(0, defer=True))
    corr_phase(2, pend=gate_phase(1, defer=True))
    corr_phase(3, pend=gate_phase(2, defer=True))
    gate_phase(3)
    MKQ("pass2_done", "sync")

    for kk in range(32):
        dma("sync", yT[:, kk, :], h_y3.ap()[:, 128 * kk:128 * (kk + 1)],
            transpose=True, waits=[("dx", "pass2_done")], mark=f"yTp_{kk}")
    it = 0
    for nb in range(4):
        for grp in range(2):
            for kg in range(4):
                wo = wo_a if it % 2 == 0 else wo_b
                w = [("pe", f"wou_{it - 2}")] if it >= 2 else [("dx", "pass2_done")]
                dma("gp", wo[:, :, :],
                    i_woT.ap()[1024 * kg:1024 * (kg + 1), 512 * nb:512 * (nb + 1)]
                    .rearrange("(k p) c -> p k c", p=128),
                    waits=w, mark=f"wo_{it}")
                for j4 in range(4):
                    j = grp * 4 + j4
                    bank = pb[j4]
                    w2 = [("dw", f"wo_{it}")] + [(None, f"yTp_{8 * kg + q8}")
                                                 for q8 in range(8)]
                    if kg == 0 and (nb > 0 or grp > 0):
                        w2.append(("dve", f"aoe_{nb * 2 + grp - 1}_{j4}"))
                    for k8 in range(8):
                        kf = 8 * kg + k8
                        OP("pe", lambda e, bank=bank, kf=kf, j=j, wo=wo, k8=k8:
                           e.matmul(bank[:, 0:512], yT[:, kf, 128 * j:128 * (j + 1)],
                                    wo[:, k8, :], start=(kf == 0), stop=(kf == 31)),
                           inc=("pe", 1) if k8 == 7 else None,
                           waits=w2 if k8 == 0 else ())
                    MK(f"wog_{it}_{j4}", "pe")
                MK(f"wou_{it}", "pe")
                it += 1
            for j4 in range(4):
                j = grp * 4 + j4
                dve(lambda e, j=j, j4=j4, nb=nb:
                    e.tensor_copy(ao[:, j, 512 * nb:512 * (nb + 1)], pb[j4][:, 0:512]),
                    waits=[("pe", f"wog_{it - 1}_{j4}")], mark=f"aoe_{nb * 2 + grp}_{j4}")
    MK("ao_done", "dve")
    if debug:
        dma("sync", t_ao.ap().rearrange("(i p) d -> p i d", p=128), ao[:, :, :],
            waits=[("dve", "ao_done")], mark="t_ao_dma")
    dma("gp", wrb[:, :, :], i_wr.ap().rearrange("(k p) c -> p k c", p=128),
        waits=[("pe", "gau_15")], mark="wr_ld")
    def rt_tile(j):
        bank = pb[4 + (j % 2)]
        w = [("dw", "wr_ld"), ("act", f"x2e_{j}")]
        if j >= 2:
            w.append(("act", f"pre_{j - 2}"))
        mmg([(bank[:, 0:128], xlnT2[:, k, 128 * j:128 * (j + 1)], wrb[:, k, :])
             for k in range(KH)], waits=w, mark=f"rtm_{j}")
        dve(lambda e, bank=bank: e.reduce_max(rtc[:, :], bank[:, 0:128], axis=AX.X,
                                              negate=True),
            waits=[("pe", f"rtm_{j}")] + ([("dve", "mix_done")] if j == 0 else []),
            mark=f"rmx_{j}")
        act(lambda e, bank=bank, j=j: e.activation(probs[:, j, :], bank[:, 0:128],
                                                   AF.Exp, bias=rtc[:, :],
                                                   accum_out=s4c[:, :]),
            waits=[("dve", f"rmx_{j}")], mark=f"pre_{j}")
        dve(lambda e: e.reciprocal(rtc[:, :], s4c[:, :]),
            waits=[("act", f"pre_{j}")], mark=f"rc1_{j}")
        dve(lambda e, j=j: e.tensor_scalar(probs[:, j, :], probs[:, j, :], rtc[:, :],
                                           None, ALU.mult),
            waits=[("dve", f"rc1_{j}")], mark=f"pn_{j}")
        dve(lambda e, j=j: e.max(out=w8[:, :], in_=probs[:, j, :]),
            waits=[("dve", f"pn_{j}")], mark=f"mx8_{j}")
        dve(lambda e: e.reduce_sum(s4c[:, :], w8[:, 0:4], axis=AX.X),
            waits=[("dve", f"mx8_{j}")], mark=f"rs_{j}")
        dve(lambda e: e.tensor_scalar(s4c[:, :], s4c[:, :], EPS_MOE, None, ALU.add),
            waits=[("dve", f"rs_{j}")], mark=f"s4a_{j}")
        dve(lambda e: e.reciprocal(rtc[:, :], s4c[:, :]),
            waits=[("dve", f"s4a_{j}")], mark=f"rc2_{j}")
        dve(lambda e, j=j: e.tensor_scalar(mskt[:, :], probs[:, j, :], w8[:, 3:4],
                                           None, ALU.is_ge),
            waits=[("dve", f"mx8_{j}")], mark=f"mk_{j}")
        dve(lambda e, j=j: e.scalar_tensor_tensor(ctm[:, j, :], probs[:, j, :],
                                                  rtc[:, :], mskt[:, :],
                                                  ALU.mult, ALU.mult),
            waits=[("dve", f"rc2_{j}"), ("dve", f"mk_{j}")], mark=f"ctm_{j}")
        w = [("dve", f"ctm_{j}")]
        if j > 0:
            w.append(("dve", f"cfm2_{j - 1}"))
        tpose(pT[:, 0:128], ctm[:, j, :], idmb[:, :], waits=w, mark=f"cT2_{j}")
        dve(lambda e, j=j: e.tensor_copy(cfm[:, j, :], pT[:, 0:128]),
            waits=[("pe", f"cT2_{j}")], mark=f"cfm2_{j}")
        mmg([(pb[3][:, 8 * j:8 * (j + 1)], cfm[:, j, :], mixs[:, :])],
            waits=[("dve", f"cfm2_{j}"), ("dve", "mix_done")], mark=f"cbm_{j}")
    for j in range(TT):
        w = [("pe", "wou_31")]
        if j >= 2:
            w.append(("dve", f"x1_{j - 2}"))
        dma("sync", xr[:, j % 2, :], i_x.ap()[128 * j:128 * (j + 1), :], waits=w,
            mark=f"xr_{j}")
        fw = [("dve", "ao_done")] + ([("act", f"xl2_{j - 2}")] if j >= 2 else [])
        fwa = list(fw) + ([("dx", f"xl2w_{j - 2}"), ("pe", f"x2p_{j - 2}")]
                          if j >= 2 else [])
        ln_stats(ao[:, j, :], xl2t[:, j % 2, :], fw, fwa, f"l2_{j}", cs=j % 2)
        rc8, bc8 = COLS[j % 2][4], COLS[j % 2][5]
        if debug:
            dve(lambda e, j=j: e.tensor_copy(dcols[:, j, 0:1], mcol[:, :]),
                waits=[("dve", f"l2_{j}b")])
            dve(lambda e, j=j: e.tensor_copy(dcols[:, j, 1:2], vcol[:, :]))
            dve(lambda e, j=j: e.tensor_copy(dcols[:, j, 2:3], ucol[:, :]))
            dve(lambda e, j=j: e.tensor_copy(dcols[:, j, 3:4], rcol[:, :]))
            dve(lambda e, j=j: e.tensor_copy(dcols[:, j, 4:5], bcol[:, :]))
            dve(lambda e, j=j: e.tensor_copy(dcols[:, j, 5:6], scol[:, :]))
        dve(lambda e, j=j, bc8=bc8: e.tensor_scalar(x1t[:, j % 2, :], xr[:, j % 2, :],
                                                    bc8[:, :], None, ALU.add),
            waits=[("dx", f"xr_{j}"), ("dve", f"l2_{j}b")]
                  + ([("dx", f"x1w_{j - 2}"), ("act", f"xl2_{j - 2}")] if j >= 2 else []),
            mark=f"xoff_{j}")
        dve(lambda e, j=j, rc8=rc8: e.scalar_tensor_tensor(x1t[:, j % 2, :], ao[:, j, :],
                                                           rc8[:, :], x1t[:, j % 2, :],
                                                           ALU.mult, ALU.add),
            waits=[("dve", f"xoff_{j}")], mark=f"x1_{j}")
        dma("sync", h_x1.ap()[128 * j:128 * (j + 1), :], x1t[:, j % 2, :],
            waits=[("dve", f"x1_{j}")], mark=f"x1w_{j}")
        if debug and j < 2:
            dma("sync", t_xr.ap()[:, HID * j:HID * (j + 1)], xr[:, j % 2, :],
                waits=[("dve", f"x1_{j}")])
            dma("sync", t_x1t.ap()[:, HID * j:HID * (j + 1)], x1t[:, j % 2, :],
                waits=[("dve", f"x1_{j}")])
        fw2 = [("dve", f"x1_{j}")]
        fwa2 = list(fw2) + ([("dx", f"xl2w_{j - 2}")] if j >= 2 else [])
        ln_stats(x1t[:, j % 2, :], xl2t[:, j % 2, :], fw2, fwa2, f"l2s_{j}", cs=j % 2)
        aw = [("dve", f"l2s_{j}b")]
        if j >= 2:
            aw.append(("dx", f"xl2w_{j - 2}"))
        act(lambda e, j=j, rc8=rc8, bc8=bc8:
            e.activation(xl2t[:, j % 2, :], x1t[:, j % 2, :], AF.Identity,
                         bias=bc8[:, :], scale=rc8[:, :]),
            waits=aw, mark=f"xl2_{j}")
        dma("gp", h_xln2.ap()[128 * j:128 * (j + 1), :], xl2t[:, j % 2, :],
            waits=[("act", f"xl2_{j}")], mark=f"xl2w_{j}")
        for kk in range(KH):
            bank = pb[kk % 3]
            w = [("act", f"xl2_{j}")]
            if j == 0 and kk < 3:
                w.append(("dve", "ao_done"))
            if kk >= 3:
                w.append(("act", f"x2e_{j}_{kk - 3}"))
            elif j > 0:
                w.append(("act", f"x2e_{j - 1}_{13 + kk}"))
            tpose(bank[:, 0:128], xl2t[:, j % 2, 128 * kk:128 * (kk + 1)], idmf[:, :],
                  waits=w, mark=f"x2t_{j}_{kk}")
            act(lambda e, bank=bank, kk=kk, j=j:
                e.activation(xlnT2[:, kk, 128 * j:128 * (j + 1)], bank[:, 0:128],
                             AF.Copy),
                waits=[("pe", f"x2t_{j}_{kk}")], mark=f"x2e_{j}_{kk}")
        MK(f"x2p_{j}", "pe")
        MK(f"x2e_{j}", "act")
        if j == 2:
            dve(lambda e: e.tensor_tensor(sq16[:, :], amp[:, :], amp[:, :], ALU.mult),
                waits=[("dx", "consts_ld"), ("dve", "ao_done"), ("dve", "x1_1")],
                mark="mq1")
            dve(lambda e: e.tensor_tensor(p8[:, :], sq16[:, 0:8], sq16[:, 8:16],
                                          ALU.add), waits=[("dve", "mq1")], mark="mq2")
            dve(lambda e: e.reduce_sum(pcol[:, :], p8[:, :], axis=AX.X),
                waits=[("dve", "mq2")], mark="mq3")
            dve(lambda e: e.tensor_scalar(pcol[:, :], pcol[:, :], EPS_MOE, None,
                                          ALU.add), waits=[("dve", "mq3")], mark="mq4")
            dve(lambda e: e.reciprocal(pcol[:, :], pcol[:, :]),
                waits=[("dve", "mq4")], mark="mixrc")
            dve(lambda e: e.tensor_scalar(p8[:, :], p8[:, :], pcol[:, :], esc[:, :],
                                          ALU.mult, ALU.mult),
                waits=[("dve", "mixrc")], mark="mixp")
            dve(lambda e: e.tensor_copy(mixs[:, :], p8[:, :]),
                waits=[("dve", "mixp")], mark="mix_done")
        if j >= 2:
            rt_tile(j - 2)
    rt_tile(6)
    rt_tile(7)
    MKQ("xln2_done", "gp")

    dve(lambda e: e.tensor_copy(comb[:, :, :],
                                pb[3][:, 0:64].rearrange("p (a b) -> p a b", a=TT)),
        waits=[("pe", "cbm_7")], mark="comb_done")
    if debug:
        dma("sync", t_comb.ap(), comb[:, :, :].rearrange("p a b -> p (a b)"),
            waits=[("dve", "comb_done")])
        dma("sync", t_probs.ap().rearrange("(i p) d -> p i d", p=128), probs[:, :, :],
            waits=[("dve", "ctm_7")])

    dve(lambda e: e.memset(m_sb[:, :, :], 0.0), waits=[("dve", "comb_done")], mark="m0")
    for it2 in range(16):
        kslab, hf = divmod(it2, 2)
        wg = wga if it2 % 2 == 0 else wgb
        wu = wua if it2 % 2 == 0 else wub
        col = 1024 * kslab + 512 * hf
        w = [("pe", f"slu_{it2 - 2}")] if it2 >= 2 else [("dx", "xln2_done"),
                                                         ("pe", "x2p_7"),
                                                         ("dve", "ao_done")]
        dma("gp", wg[:, :, :],
            i_wg.ap()[:, col:col + 512].rearrange("(k p) c -> p k c", p=128), waits=w,
            mark=f"slwg_{it2}")
        dma("gp", wu[:, :, :],
            i_wu.ap()[:, col:col + 512].rearrange("(k p) c -> p k c", p=128),
            mark=f"slw_{it2}")
        for j in range(TT):
            gb = pb[2 * (j % 2)]
            ub = pb[2 * (j % 2) + 1]
            gg = it2 * 8 + j
            w2 = [("dw", f"slw_{it2}"), ("dw", f"slwg_{it2}"), ("act", f"x2e_{j}")]
            if gg >= 2:
                w2.append(("dve", f"bas_{gg - 2}"))
            else:
                w2.append(("act", "x2e_7"))
            mmg([(gb[:, 0:512], xlnT2[:, k, 128 * j:128 * (j + 1)], wg[:, k, :])
                 for k in range(KH)], waits=w2, mark=f"gm_{gg}")
            mmg([(ub[:, 0:512], xlnT2[:, k, 128 * j:128 * (j + 1)], wu[:, k, :])
                 for k in range(KH)], mark=f"um_{gg}")
            aw = [("pe", f"gm_{gg}")]
            if gg >= 2:
                aw.append(("dve", f"bas_{gg - 2}"))
            act(lambda e, gb=gb, gg=gg: e.activation(sg1[:, gg % 2, :], gb[:, 0:512],
                                                     AF.Silu),
                waits=aw, mark=f"sl_{gg}")
            dve(lambda e, ub=ub, gg=gg: e.tensor_tensor(bas1[:, gg % 2, :],
                                                        ub[:, 0:512], sg1[:, gg % 2, :],
                                                        ALU.mult),
                waits=[("act", f"sl_{gg}"), ("pe", f"um_{gg}")], mark=f"bt_{gg}")
            dve(lambda e, gg=gg, j=j, kslab=kslab, hf=hf:
                e.scalar_tensor_tensor(m_sb[:, j, 512 * hf:512 * (hf + 1)],
                                       bas1[:, gg % 2, :], comb[:, j, kslab:kslab + 1],
                                       m_sb[:, j, 512 * hf:512 * (hf + 1)],
                                       ALU.mult, ALU.add),
                waits=[("dve", "m0"), ("dve", f"bt_{gg}")], mark=f"bas_{gg}")
        MK(f"slu_{it2}", "pe")
    MK("m_final", "dve")
    dma("gp", h_m.ap().rearrange("(i p) d -> p i d", p=128), m_sb[:, :, :],
        waits=[("dve", "m_final")], mark="m_sp")
    for nb in range(4):
        wg = wga if nb % 2 == 0 else wgb
        wu = wua if nb % 2 == 0 else wub
        w = [("pe", f"shu_{nb - 2}")] if nb >= 2 else [("pe", f"slu_{14 + nb}")]
        dma("gp", wg[:, :, :],
            i_swg.ap()[:, 512 * nb:512 * (nb + 1)].rearrange("(k p) c -> p k c", p=128),
            waits=w, mark=f"shwg_{nb}")
        dma("gp", wu[:, :, :],
            i_swu.ap()[:, 512 * nb:512 * (nb + 1)].rearrange("(k p) c -> p k c", p=128),
            mark=f"shw_{nb}")
        for j in range(TT):
            gb = pb[2 * (j % 2)]
            ub = pb[2 * (j % 2) + 1]
            gg = nb * 8 + j
            w2 = [("dw", f"shw_{nb}"), ("dw", f"shwg_{nb}"), ("dw", "m_sp")]
            if gg >= 2:
                w2.append(("dve", f"sme_{gg - 2}"))
            mmg([(gb[:, 0:512], xlnT2[:, k, 128 * j:128 * (j + 1)], wg[:, k, :])
                 for k in range(KH)], waits=w2, mark=f"sgm_{gg}")
            mmg([(ub[:, 0:512], xlnT2[:, k, 128 * j:128 * (j + 1)], wu[:, k, :])
                 for k in range(KH)], mark=f"sum_{gg}")
            aw = [("pe", f"sgm_{gg}")]
            if gg >= 2:
                aw.append(("dve", f"sme_{gg - 2}"))
            act(lambda e, gb=gb, gg=gg: e.activation(sg1[:, gg % 2, :], gb[:, 0:512],
                                                     AF.Silu),
                waits=aw, mark=f"ss_{gg}")
            dve(lambda e, ub=ub, gg=gg, j=j, nb=nb:
                e.tensor_tensor(smid[:, j, 512 * nb:512 * (nb + 1)], ub[:, 0:512],
                                sg1[:, gg % 2, :], ALU.mult),
                waits=[("act", f"ss_{gg}"), ("pe", f"sum_{gg}")], mark=f"sme_{gg}")
        MK(f"shu_{nb}", "pe")
    MK("smid_done", "dve")
    for j in range(TT):
        dma("sync", h_smid.ap()[128 * j:128 * (j + 1), :], smid[:, j, :],
            waits=[("dve", f"sme_{24 + j}"), ("act", f"x2e_{j}")])
    MKQ("smid_sp", "sync")
    for kk in range(8):
        dma("sync", mT[:, kk, :], h_m.ap()[:, 128 * kk:128 * (kk + 1)],
            transpose=True, waits=[("dw", "m_sp"), ("pe", "shu_3")], mark=f"mTp_{kk}")
    marks["mT_ld"] = [p for kk in range(8) for p in marks[f"mTp_{kk}"]]
    for kk in range(KH):
        dma("sync", smidT[:, kk, :], h_smid.ap()[:, 128 * kk:128 * (kk + 1)],
            transpose=True, waits=[("dx", "smid_sp")], mark=f"smTp_{kk}")
    marks["smidT_ld"] = [p for kk in range(KH) for p in marks[f"smTp_{kk}"]]

    for nb in range(4):
        wd_t = wdb_a if nb % 2 == 0 else wdb_b
        sw_t = swda if nb % 2 == 0 else swdb
        w = [("pe", f"dnu_{nb - 2}")] if nb >= 2 else [("dx", "smid_sp"),
                                                       ("pe", "shu_3")]
        dma("gp", wd_t[:, :, :],
            i_wd.ap()[:, 512 * nb:512 * (nb + 1)].rearrange("(k p) c -> p k c", p=128),
            waits=w, mark=f"dnwd_{nb}")
        dma("gp", sw_t[:, :, :],
            i_swd.ap()[:, 512 * nb:512 * (nb + 1)].rearrange("(k p) c -> p k c", p=128),
            mark=f"dnw_{nb}")
        for grp in range(2):
            gidx = nb * 2 + grp
            for j4 in range(4):
                j = grp * 4 + j4
                bank = pb[j4]
                w2 = [("dw", f"dnw_{nb}"), ("dw", f"dnwd_{nb}"), ("dx", "mT_ld"),
                      ("dx", "smidT_ld")]
                if gidx >= 1:
                    w2.append(("dve", f"dne_{gidx - 1}_{j4}"))
                lst = [(bank[:, 0:512], mT[:, k, 128 * j:128 * (j + 1)], wd_t[:, k, :])
                       for k in range(8)]
                lst += [(bank[:, 0:512], smidT[:, k, 128 * j:128 * (j + 1)], sw_t[:, k, :])
                        for k in range(KH)]
                mmg(lst, waits=w2, mark=f"dnm_{gidx}_{j4}")
            for j4 in range(4):
                j = grp * 4 + j4
                dve(lambda e, j=j, j4=j4, nb=nb, gidx=gidx:
                    e.tensor_copy(ffo[:, j, 512 * nb:512 * (nb + 1)], pb[j4][:, 0:512]),
                    waits=[("pe", f"dnm_{gidx}_{j4}")], mark=f"dne_{gidx}_{j4}")
        MK(f"dnu_{nb}", "pe")
    MK("down_all", "pe")
    MK("ffo_done", "dve")
    if debug:
        dma("sync", t_ffo.ap().rearrange("(i p) d -> p i d", p=128), ffo[:, :, :],
            waits=[("dve", "ffo_done")])
        dma("sync", t_cols.ap(), dcols[:, :, :].rearrange("p a b -> p (a b)"),
            waits=[("dve", "x1_7")])

    for j in range(TT):
        w = [("pe", "down_all"), ("dx", f"x1w_{j}")]
        if j >= 2:
            w.append(("dve", f"out_{j - 2}"))
        dma("sync", x1r[:, j % 2, :], h_x1.ap()[128 * j:128 * (j + 1), :], waits=w,
            mark=f"x1r_{j}")
        fw3 = [("dve", "ffo_done")]
        ln_stats(ffo[:, j, :], junk10[:, :], fw3, fw3, f"l3_{j}", cs=j % 2)
        rc0, bc0 = COLS[j % 2][4], COLS[j % 2][5]
        wq = [("dx", f"x1r_{j}"), ("dve", f"l3_{j}b")]
        if j >= 2:
            wq.append(("dx", f"outw_{j - 2}"))
        dve(lambda e, j=j, bc0=bc0: e.tensor_scalar(x2t[:, j % 2, :], x1r[:, j % 2, :],
                                                    bc0[:, :], None, ALU.add),
            waits=wq, mark=f"xo3_{j}")
        dve(lambda e, j=j, rc0=rc0: e.scalar_tensor_tensor(x2t[:, j % 2, :], ffo[:, j, :],
                                                           rc0[:, :], x2t[:, j % 2, :],
                                                           ALU.mult, ALU.add),
            waits=[("dve", f"xo3_{j}")], mark=f"out_{j}")
        MK(f"l3a_{j}", "dve")
        dma("sync", o_y.ap()[128 * j:128 * (j + 1), :], x2t[:, j % 2, :],
            waits=[("dve", f"out_{j}")], mark=f"outw_{j}")
    for i in range(NDMASEM):
        v = cnt[f"dx{i}"]
        if v > 0:
            OP("sync", lambda e, i=i, v=v: e.wait_ge(sh[f"dx{i}"], v), inc=None)

    def emit_for(engname):
        def body(eng):
            wm = {}
            for (en, fn, inc, waits) in ops:
                if en != engname:
                    continue
                for (s, v) in waits:
                    if isinstance(v, str):
                        pairs = marks[v]
                    else:
                        pairs = [(s, v)]
                    for (s2, vv) in pairs:
                        if vv <= 0:
                            continue
                        if vv > wm.get(s2, 0):
                            wm[s2] = vv
                            eng.wait_ge(sh[s2], vv)
                instr = fn(eng)
                if inc is not None and instr is not None:
                    instr.then_inc(sh[inc[0]], inc[1])
        return body

    with nc.Block() as block:
        block.sync(emit_for("sync"))
        block.gpsimd(emit_for("gp"))
        block.tensor(emit_for("pe"))
        block.vector(emit_for("dve"))
        block.scalar(emit_for("act"))
    return nc


_NC_CACHE = {}


def _get_nc():
    if "nc" not in _NC_CACHE:
        _NC_CACHE["nc"] = build_nc()
    return _NC_CACHE["nc"]


def _host_shared(inputs):
    f32 = np.float32
    g = {}
    g["wT"] = np.ascontiguousarray(np.asarray(inputs["L0_attn_in_proj_w"], f32).T)
    cw = np.asarray(inputs["L0_attn_conv_w"], f32)
    cd = np.zeros((NCT, 4, 128, 128), f32)
    idx = np.arange(128)
    for m in range(NCT):
        for k in range(4):
            cd[m, k, idx, idx] = cw[128 * m + idx, 0, k]
    g["cdiag"] = cd
    g["convb"] = np.asarray(inputs["L0_attn_conv_b"], f32).reshape(1, CONVD)
    g["alog"] = np.ascontiguousarray(np.asarray(inputs["L0_attn_A_log"], f32).reshape(HEADS, 1))
    g["dtb"] = np.broadcast_to(np.asarray(inputs["L0_attn_dt_bias"], f32), (128, HEADS)).copy()
    g["dd"] = np.broadcast_to(np.asarray(inputs["L0_attn_D"], f32), (128, HEADS)).copy()
    g["nw"] = np.broadcast_to(np.asarray(inputs["L0_attn_norm_w"], f32), (128, INTER)).copy()
    g["woT"] = np.ascontiguousarray(np.asarray(inputs["L0_attn_out_proj_w"], f32).T)
    g["wr"] = np.ascontiguousarray(np.asarray(inputs["L0_ffn_w_router"], f32))
    g["wg"] = np.ascontiguousarray(np.asarray(inputs["L0_ffn_w_gate"], f32))
    g["wu"] = np.ascontiguousarray(np.asarray(inputs["L0_ffn_w_up"], f32))
    g["wd"] = np.ascontiguousarray(np.asarray(inputs["L0_ffn_w_down"], f32))
    g["swg"] = np.ascontiguousarray(np.asarray(inputs["L0_ffn_shared_w_gate"], f32))
    g["swu"] = np.ascontiguousarray(np.asarray(inputs["L0_ffn_shared_w_up"], f32))
    g["swd"] = np.ascontiguousarray(np.asarray(inputs["L0_ffn_shared_w_down"], f32))
    a = np.asarray(inputs["L0_ffn_amplitudes"], f32)
    g["amp"] = np.ascontiguousarray(np.concatenate([a[:, :, 0], a[:, :, 1]], axis=1))
    g["esc"] = np.ascontiguousarray(np.asarray(inputs["L0_ffn_expert_scale"], f32).reshape(NE, 1))
    g["idmb"] = np.eye(128, dtype=ml_dtypes.bfloat16)
    g["idmf"] = np.eye(128, dtype=f32)
    tri = np.where(np.arange(128)[None, :] >= np.arange(128)[:, None], BIG, -BIG).astype(f32)
    g["dcapa"] = np.ascontiguousarray(np.concatenate([tri, np.full((128, 128), BIG, f32)], axis=1))
    g["dcapb"] = np.ascontiguousarray(tri)
    g["onesr"] = np.ones((1, 128), f32)
    sel = np.zeros((HEADS, HEADS * 128), f32)
    for hh in range(HEADS):
        sel[hh, hh * 128:(hh + 1) * 128] = 1.0
    g["selb"] = sel
    return g


def _host_percore(core, inputs):
    f32 = np.float32
    x = np.asarray(inputs["L0_attn_hidden_states"], f32)
    b, h = core // 2, core % 2
    xs = np.zeros((TPAD, HID), f32)
    xs[0:TLOC] = x[b, h * TLOC:(h + 1) * TLOC]
    if h == 1:
        xs[TLOC:TLOC + 3] = x[b, TLOC - 3:TLOC]
    mk = np.zeros((128, 8), f32)
    if h == 1:
        mk[:, core - 1] = 1.0
    return {"xin": xs, "maskv": mk}


def kernel(**inputs):
    f32 = np.float32
    nc = _get_nc()
    shared = _host_shared(inputs)
    in_maps = []
    for c in range(8):
        m = dict(shared)
        m.update(_host_percore(c, inputs))
        in_maps.append(m)
    res = run_bass_kernel_spmd(nc, in_maps, core_ids=list(range(8)))
    out = np.zeros((4, 2048, HID), f32)
    for c in range(8):
        b, h = c // 2, c % 2
        out[b, h * TLOC:(h + 1) * TLOC] = res.results[c]["yout"]
    return out


# version 11
# speedup vs baseline: 1.0174x; 1.0174x over previous
import numpy as np
import ml_dtypes
import concourse.bass as bass
import concourse.mybir as mybir
from concourse.bass_utils import run_bass_kernel_spmd

F32 = mybir.dt.float32
BF16 = mybir.dt.bfloat16
AF = mybir.ActivationFunctionType
ALU = mybir.AluOpType
AX = mybir.AxisListType

HID = 2048
GATE = 4096
CONVD = 4352
INTER = 4096
HEADS = 32
CH = 256
NCH = 4
TT = 8
TLOC = 1024
TPAD = 1152
NCT = 34
KH = 16
EPS_LN = 1e-5
EPS_RMS = 1e-5
EPS_MOE = 1e-8
NE = 128
KE = 8
FI = 1024
H2 = 2048
BIG = 1.0e30
NDMASEM = 20
NDSSEM = 8
SEMS = (("pe", "dve", "act", "gp", "cc")
        + tuple(f"dx{i}" for i in range(NDMASEM))
        + tuple(f"dw{i}" for i in range(NDMASEM))
        + tuple(f"ds{i}" for i in range(NDSSEM)))


def build_nc(debug=False):
    nc = bass.Bass("TRN2", target_bir_lowering=False)
    d = nc.dram_tensor

    i_x = d("xin", [TPAD, HID], F32, kind="ExternalInput")
    i_wT = d("wT", [HID, 8480], F32, kind="ExternalInput")
    i_cdiag = d("cdiag", [NCT, 4, 128, 128], F32, kind="ExternalInput")
    i_convb = d("convb", [1, CONVD], F32, kind="ExternalInput")
    i_alog = d("alog", [HEADS, 1], F32, kind="ExternalInput")
    i_dtb = d("dtb", [128, HEADS], F32, kind="ExternalInput")
    i_dd = d("dd", [128, HEADS], F32, kind="ExternalInput")
    i_nw = d("nw", [128, INTER], F32, kind="ExternalInput")
    i_woT = d("woT", [INTER, HID], F32, kind="ExternalInput")
    i_wr = d("wr", [HID, NE], F32, kind="ExternalInput")
    i_wg = d("wg", [HID, KE * FI], F32, kind="ExternalInput")
    i_wu = d("wu", [HID, KE * FI], F32, kind="ExternalInput")
    i_wd = d("wd", [FI, HID], F32, kind="ExternalInput")
    i_swg = d("swg", [HID, H2], F32, kind="ExternalInput")
    i_swu = d("swu", [HID, H2], F32, kind="ExternalInput")
    i_swd = d("swd", [H2, HID], F32, kind="ExternalInput")
    i_amp = d("amp", [NE, 16], F32, kind="ExternalInput")
    i_esc = d("esc", [NE, 1], F32, kind="ExternalInput")
    i_idmb = d("idmb", [128, 128], BF16, kind="ExternalInput")
    i_idmf = d("idmf", [128, 128], F32, kind="ExternalInput")
    i_dcapa = d("dcapa", [128, 256], F32, kind="ExternalInput")
    i_dcapb = d("dcapb", [128, 128], F32, kind="ExternalInput")
    i_ones = d("onesr", [1, 128], F32, kind="ExternalInput")
    i_selb = d("selb", [HEADS, HEADS * 128], F32, kind="ExternalInput")
    i_mask = d("maskv", [128, 8], F32, kind="ExternalInput")
    o_y = d("yout", [TLOC, HID], F32, kind="ExternalOutput")

    kd = "ExternalOutput" if debug else "Internal"
    h_xln = d("h_xln", [TPAD, HID], BF16, kind=kd)
    h_sg = d("h_sg", [TLOC, INTER], BF16, kind=kd)
    h_ytot = d("h_ytot", [TLOC, INTER], BF16, kind=kd)
    h_y3 = d("h_y3", [TLOC, INTER], BF16, kind=kd)
    h_x1 = d("h_x1", [TLOC, HID], F32, kind=kd)
    h_xln2 = d("h_xln2", [TLOC, HID], BF16, kind=kd)
    h_m = d("h_m", [TLOC, FI], BF16, kind=kd)
    h_smid = d("h_smid", [TLOC, H2], BF16, kind=kd)
    h_sout = d("h_sout", [128, HEADS * 128], BF16)
    h_sgat = d("h_sgat", [8 * 128, HEADS * 128], BF16)
    if debug:
        t_convo = d("t_convo", [TLOC, CONVD], BF16, kind="ExternalOutput")
        t_dt = d("t_dt", [128, TT * HEADS], F32, kind="ExternalOutput")
        t_ac = d("t_ac", [32, TLOC], F32, kind="ExternalOutput")
        t_ao = d("t_ao", [TLOC, HID], F32, kind="ExternalOutput")
        t_ffo = d("t_ffo", [TLOC, HID], F32, kind="ExternalOutput")
        t_comb = d("t_comb", [128, TT * KE], F32, kind="ExternalOutput")
        t_probs = d("t_probs", [TLOC, NE], F32, kind="ExternalOutput")
        t_sout = d("t_sout", [128, HEADS * 128], BF16, kind="ExternalOutput")
        t_cols = d("t_cols", [128, TT * 6], F32, kind="ExternalOutput")
        t_xr = d("t_xr", [128, 2 * HID], F32, kind="ExternalOutput")
        t_x1t = d("t_x1t", [128, 2 * HID], F32, kind="ExternalOutput")

    ARENA = 204 * 1024
    arena = nc.alloc_sbuf_tensor("arena", [128, ARENA], mybir.dt.uint8)
    base = nc.lookup_mloc(arena).addr

    def at(name, shape, dtype, off):
        sz = int(np.prod(shape[1:])) * mybir.dt.size(dtype)
        assert off + sz <= ARENA, f"{name} overflows arena: {off}+{sz}"
        return nc.alloc_sbuf_tensor_at(name, shape, dtype, offset=base + off)

    A, B, C, D = 0, 73728, 139264, 176128
    x_sb = at("x_sb", [128, 9, HID], F32, A)
    convo = at("convo", [128, TT, CONVD], BF16, A)
    ysb = at("ysb", [128, 2, 2, INTER], BF16, A + 0)
    y2 = at("y2", [128, 2, INTER], BF16, A + 32768)
    y3t = at("y3t", [128, 2, INTER], BF16, A + 49152)
    junkb = at("junkb", [128, INTER], BF16, A + 65536)
    argm = at("argm", [128, 384], F32, A + 69632)
    e_sb = at("e_sb", [128, 384], BF16, A + 71168)
    mt_sb = at("mt_sb", [128, 384], BF16, A + 71936)
    erow = at("erow", [128, 256], BF16, A + 72704)
    ceb = at("ceb", [128, 256], BF16, A + 73216)
    yT = at("yT", [128, 32, TLOC], BF16, A)
    xr = at("xr", [128, 2, HID], F32, A + 0)
    x1t = at("x1t", [128, 2, HID], F32, A + 16384)
    xl2t = at("xl2t", [128, 2, HID], F32, A + 32768)
    wga = at("wga", [128, KH, 512], BF16, A + 0)
    wua = at("wua", [128, KH, 512], BF16, A + 16384)
    wgb = at("wgb", [128, KH, 512], BF16, A + 32768)
    wub = at("wub", [128, KH, 512], BF16, A + 49152)
    sg1 = at("sg1", [128, 2, 512], BF16, A + 65536)
    bas1 = at("bas1", [128, 2, 512], BF16, A + 67584)
    ffo = at("ffo", [128, TT, HID], F32, A)
    xln_sb = at("xln_sb", [128, 9, HID], BF16, B + 0)
    zrow = at("zrow", [128, HID], BF16, B + 36864)
    hbc = at("hbc", [128, 3, 1032], BF16, B + 0)
    convb_bf = at("convb_bf", [1, CONVD], BF16, B + 40960)
    gwa = at("gwa", [128, KH, 256], BF16, B + 40960)
    S = at("S", [128, HEADS * 128], F32, B + 0)
    S_bf = at("S_bf", [128, HEADS * 128], BF16, B + 16384)
    ytot = at("ytot", [128, 2, INTER], BF16, B + 24576)
    dt_fm = at("dt_fm", [32, TLOC], F32, B + 40960)
    ad_fm = at("ad_fm", [32, TLOC], F32, B + 45056)
    ac_fm = at("ac_fm", [32, TLOC], F32, B + 49152)
    ac_hi = at("ac_hi", [32, TLOC], BF16, B + 53248)
    ac_lo = at("ac_lo", [32, TLOC], BF16, B + 55296)
    dcy_fm = at("dcy_fm", [32, TLOC], F32, B + 57344)
    zeros32 = at("zeros32", [32, CH], F32, B + 61440)
    srow = at("srow", [1, HEADS], F32, B + 62464)
    b_fm = at("b_fm", [128, TT, 128], BF16, B + 61440)
    c_fm = at("c_fm", [128, TT, 128], BF16, B + 63488)
    S_in = at("S_in", [128, HEADS * 128], BF16, B + 0)
    nw_bf = at("nw_bf", [128, INTER], BF16, B + 8192)
    sgat8 = at("sgat8", [128, 2, HEADS * 128], BF16, B + 16384)
    sgc = at("sgc", [128, 2, 2, INTER], BF16, B + 32768)
    ao = at("ao", [128, TT, HID], F32, B)
    probs = at("probs", [128, TT, NE], F32, B + 0)
    ctm = at("ctm", [128, TT, NE], BF16, B + 4096)
    cfm = at("cfm", [128, TT, NE], BF16, B + 6144)
    comb = at("comb", [128, TT, KE], F32, B + 8192)
    sq16 = at("sq16", [128, 16], F32, B + 8512)
    p8 = at("p8", [128, KE], F32, B + 8576)
    pcol = at("pcol", [128, 1], F32, B + 8640)
    m_sb = at("m_sb", [128, TT, FI], F32, B + 12288)
    smid = at("smid", [128, TT, H2], BF16, B + 0)
    smidT = at("smidT", [128, KH, TLOC], BF16, B + 32768)
    swda = at("swda", [128, KH, 512], BF16, B + 0)
    swdb = at("swdb", [128, KH, 512], BF16, B + 16384)
    x1r = at("x1r", [128, 2, HID], F32, B + 32768)
    junk10 = at("junk10", [128, HID], F32, B + 0)
    xlnT = at("xlnT", [128, KH, TPAD], BF16, C)
    c_fm2 = at("c_fm2", [128, TT, 128], BF16, C + 0)
    ysc2 = at("ysc2", [128, 2, 256], F32, C + 7168)
    ecq = at("ecq", [128, 2, 2, HEADS], F32, C + 9216)
    q_bc = at("q_bc", [128, NCH, HEADS], F32, C + 13824)
    wo_a = at("wo_a", [128, 8, 512], BF16, C + 0)
    wo_b = at("wo_b", [128, 8, 512], BF16, C + 8192)
    xlnT2 = at("xlnT2", [128, KH, TLOC], BF16, C + 0)
    mT = at("mT", [128, 8, TLOC], BF16, C + 0)
    wdb_a = at("wdb_a", [128, 8, 512], BF16, C + 16384)
    wdb_b = at("wdb_b", [128, 8, 512], BF16, C + 24576)
    x2t = at("x2t", [128, 2, HID], F32, C + 16384)
    doff = [D]

    def atD(name, shape, dtype):
        t = at(name, shape, dtype, doff[0])
        sz = int(np.prod(shape[1:])) * mybir.dt.size(dtype)
        doff[0] += (sz + 31) // 32 * 32
        return t

    idmb = atD("idmbS", [128, 128], BF16)
    idmf = atD("idmfS", [128, 128], F32)
    dcapa = atD("dcapaS", [128, 256], F32)
    dcapb = atD("dcapbS", [128, 128], F32)
    ones_f = atD("ones_fS", [1, 128], F32)
    ones_b = atD("ones_bS", [1, 128], BF16)
    selb = atD("selbS", [HEADS, HEADS * 128], BF16)
    dtb = atD("dtbS", [128, HEADS], F32)
    ddv = atD("ddvS", [128, HEADS], F32)
    alog = atD("alogS", [HEADS, 1], F32)
    a_e = atD("a_eS", [HEADS, 1], F32)
    esc = atD("escS", [NE, 1], F32)
    amp = atD("ampS", [NE, 16], F32)
    maskv = atD("maskvS", [128, 8], F32)
    mixs = atD("mixsS", [NE, KE], BF16)
    stat = atD("statS", [128, 24], F32)
    stat2 = atD("stat2S", [128, 2], F32)
    rcol = atD("rcolS", [128, 1], F32)
    bcol = atD("bcolS", [128, 1], F32)
    ucol = atD("ucolS", [128, 1], F32)
    s4c = atD("s4cS", [128, 1], F32)
    w8 = atD("w8S", [128, 8], F32)
    mskt = atD("msktS", [128, NE], F32)
    mcol = atD("mcolS", [128, 1], F32)
    scol = atD("scolS", [128, 1], F32)
    vcol = atD("vcolS", [128, 1], F32)
    m2col = atD("m2colS", [128, 1], F32)
    s2col = atD("s2colS", [128, 1], F32)
    v2col = atD("v2colS", [128, 1], F32)
    u2col = atD("u2colS", [128, 1], F32)
    r2col = atD("r2colS", [128, 1], F32)
    b2col = atD("b2colS", [128, 1], F32)
    ysq = atD("ysqS", [128, 1], F32)
    rtc = atD("rtcS", [128, 1], F32)
    rr2 = atD("rr2S", [128, 1], F32)
    dtraw = atD("dtrawS", [128, TT, HEADS], F32)
    dtv = atD("dtvS", [128, TT, HEADS], F32)
    sp1 = atD("sp1S", [128, TT, HEADS], F32)
    acT = atD("acTS", [128, TT, HEADS], F32)
    eaT = atD("eaTS", [128, TT, HEADS], F32)
    dT = atD("dTS", [128, TT, HEADS], F32)
    r_bc = atD("r_bcS", [128, NCH, HEADS], F32)
    wip_off = doff[0]
    wip = atD("wipS", [128, 2, KH, 128], BF16)
    cdt_off = doff[0]
    cdt = atD("cdtS", [128, 2, 4, 128], BF16)
    wdt_off = doff[0]
    wdt = atD("wdtS", [128, KH, HEADS], BF16)
    gwb = at("gwb", [128, KH, 256], BF16, wip_off)
    sgblk = at("sgblk2", [128, 2, 256], BF16, cdt_off)
    xdt = at("xdt", [128, 2, 2, 128], BF16, cdt_off + 1024)
    bdb = at("bdb", [128, 2, 2, 128], BF16, wdt_off)
    wrb = at("wrbS", [128, KH, NE], BF16, C + 32768)
    dcols = atD("dcolsS", [128, TT, 6], F32)
    assert doff[0] <= ARENA, f"D overflow {doff[0]}"

    pb = [nc.alloc_psum_tensor(f"pb{i}", [128, 512], F32) for i in range(6)]
    pT = nc.alloc_psum_tensor("pT", [128, 1024], BF16)
    pF = nc.alloc_psum_tensor("pF", [128, 128], F32)

    sh = {s: nc.semaphore(f"s_{s}").__enter__() for s in SEMS}

    ops = []
    cnt = {s: 0 for s in SEMS}
    marks = {}

    def OP(eng, fn, inc=None, waits=()):
        ops.append((eng, fn, inc, tuple(waits)))
        if inc is not None:
            cnt[inc[0]] += inc[1]

    def MK(name, sem):
        marks[name] = [(sem, cnt[sem])]

    def MKQ(name, q):
        p = "dx" if q == "sync" else "dw"
        marks[name] = [(f"{p}{i}", cnt[f"{p}{i}"]) for i in range(NDMASEM)
                       if cnt[f"{p}{i}"] > 0]

    dma_idx = {"sync": 0, "gp": 0, "act": 0}

    def dma(q, out, in_, waits=(), mark=None, **kw):
        pfx = {"sync": "dx", "gp": "dw", "act": "ds"}[q]
        n = NDSSEM if q == "act" else NDMASEM
        s = f"{pfx}{dma_idx[q] % n}"
        dma_idx[q] += 1
        OP(q, lambda e, out=out, in_=in_, kw=kw: e.dma_start(out=out, in_=in_, **kw),
           inc=(s, 16), waits=waits)
        if mark:
            MK(mark, s)
        return s

    def mmg(lst, waits=(), mark=None):
        n = len(lst)
        for q, (o, l, r) in enumerate(lst):
            OP("pe",
               lambda e, o=o, l=l, r=r, st=(q == 0), sp=(q == n - 1):
               e.matmul(o, l, r, start=st, stop=sp),
               inc=("pe", 1) if q == n - 1 else None,
               waits=waits if q == 0 else ())
        if mark:
            MK(mark, "pe")

    def tpose(out_ps, in_ap, ident, waits=(), mark=None):
        OP("pe", lambda e, o=out_ps, i=in_ap, dd=ident: e.transpose(o, i, dd),
           inc=("pe", 1), waits=waits)
        if mark:
            MK(mark, "pe")

    def dve(fn, waits=(), mark=None):
        OP("dve", fn, inc=("dve", 1), waits=waits)
        if mark:
            MK(mark, "dve")

    def act(fn, waits=(), mark=None):
        OP("act", fn, inc=("act", 1), waits=waits)
        if mark:
            MK(mark, "act")

    COLS = [(mcol, scol, vcol, ucol, rcol, bcol),
            (m2col, s2col, v2col, u2col, r2col, b2col)]

    def ln_stats(src_ap, junk_ap, dwaits, awaits, pfx, cs=0):
        mc, sc, vc, uc, rc, bc = COLS[cs]
        dve(lambda e, s=src_ap: e.reduce_sum(mc[:, :], s, axis=AX.X),
            waits=dwaits, mark=f"{pfx}m")
        act(lambda e, s=src_ap, jk=junk_ap: e.activation(jk, s, AF.Square,
                                                         accum_out=sc[:, :]),
            waits=awaits, mark=f"{pfx}s")
        dve(lambda e: e.tensor_scalar(vc[:, :], sc[:, :], 1.0 / HID, None, ALU.mult),
            waits=[("act", f"{pfx}s")])
        dve(lambda e: e.tensor_scalar(mc[:, :], mc[:, :], 1.0 / HID, None, ALU.mult),
            mark=f"{pfx}mm")
        dve(lambda e: e.scalar_tensor_tensor(uc[:, :], mc[:, :], mc[:, :],
                                             vc[:, :], ALU.mult, ALU.subtract),
            waits=[("dve", f"{pfx}mm")], mark=f"{pfx}u")
        dve(lambda e: e.tensor_scalar(uc[:, :], uc[:, :], -1.0, EPS_LN,
                                      ALU.mult, ALU.add),
            waits=[("dve", f"{pfx}u")], mark=f"{pfx}v")
        act(lambda e: e.activation(bc[:, :], uc[:, :], AF.Ln),
            waits=[("dve", f"{pfx}v")])
        act(lambda e: e.activation(rc[:, :], bc[:, :], AF.Exp, scale=-0.5),
            mark=f"{pfx}r")
        dve(lambda e: e.tensor_scalar(bc[:, :], mc[:, :], rc[:, :], -1.0,
                                      ALU.mult, ALU.mult),
            waits=[("act", f"{pfx}r")], mark=f"{pfx}b")

    dma("sync", x_sb[:, :, :], i_x.ap().rearrange("(i p) d -> p i d", p=128), mark="x_ld")
    for t, src in [(idmb, i_idmb), (idmf, i_idmf), (dcapa, i_dcapa), (dcapb, i_dcapb),
                   (ones_f, i_ones), (alog, i_alog), (dtb, i_dtb), (ddv, i_dd),
                   (esc, i_esc), (amp, i_amp), (maskv, i_mask)]:
        dma("sync", t[tuple(slice(None) for _ in t.shape)], src.ap())
    MKQ("consts_ld", "sync")
    dma("gp", selb[:, :], i_selb.ap())
    dma("gp", ones_b[:, :], i_ones.ap())
    dma("gp", wdt[:, :, :], i_wT.ap()[:, 8448:8480].rearrange("(k p) c -> p k c", p=128),
        mark=None)
    MKQ("smallw_ld", "gp")

    dve(lambda e: e.memset(zrow[:, :], 0.0), waits=[("dx", "x_ld")], mark="zrow")
    for i in range(9):
        fw = [("dx", "x_ld"), ("dx", "consts_ld")]
        fwa = list(fw) + ([("act", f"ln1x_{i - 2}")] if i > 1 else [])
        ln_stats(x_sb[:, i, :], xln_sb[:, i, :],
                 fw + ([("act", f"ln1x_{i - 2}")] if i > 1 else []), fwa,
                 f"ln1_{i}", cs=i % 2)
        rc1, bc1 = COLS[i % 2][4], COLS[i % 2][5]
        act(lambda e, i=i, rc1=rc1, bc1=bc1:
            e.activation(xln_sb[:, i, :], x_sb[:, i, :], AF.Identity,
                         bias=bc1[:, :], scale=rc1[:, :]),
            waits=[("dve", f"ln1_{i}b")], mark=f"ln1x_{i}")
    MK("xln_done", "act")
    dma("sync", h_xln.ap()[3:3 + TLOC, :].rearrange("(i p) d -> p i d", p=128),
        xln_sb[:, 0:8, :], waits=[("act", "xln_done")])
    dma("sync", h_xln.ap()[0:3, :], xln_sb[0:3, 8, :])
    dma("sync", h_xln.ap()[1027:TPAD, :], zrow[0:125, :], waits=[("dve", "zrow")],
        mark=None)
    MKQ("xlnh", "sync")
    for kk in range(KH):
        dma("sync", xlnT[:, kk, :], h_xln.ap()[:, 128 * kk:128 * (kk + 1)],
            transpose=True, waits=[("dx", "xlnh")])
    MKQ("xlnT", "sync")
    dma("gp", convb_bf[:, :], i_convb.ap(), waits=[("dx", "xlnh")], mark="convb_ld")

    NBS = [(0, 512), (512, 512), (1024, 8)]
    for m in range(NCT):
        wts = [("pe", f"ipu_{m - 2}")] if m >= 2 else [("dx", "xlnh")]
        dma("gp", wip[:, m % 2, :, :],
            i_wT.ap()[:, GATE + 128 * m:GATE + 128 * (m + 1)].rearrange("(k p) c -> p k c", p=128),
            waits=wts, mark=f"wip_{m}")
        cwts = [("pe", f"cvu_{m - 2}")] if m >= 2 else []
        dma("gp", cdt[:, m % 2, :, :], i_cdiag.ap()[m, :, :, :].rearrange("k p c -> p k c"),
            waits=cwts, mark=f"cdt_{m}")
        for nb, (o0, nn) in enumerate(NBS):
            g = m * 3 + nb
            w = [("dw", f"wip_{m}"), ("dx", "xlnT")]
            if g >= 2:
                gm, gn = divmod(g - 2, 3)
                w.append(("dve", f"ipe_{gm}_{gn}"))
            bank = pb[g % 2]
            mmg([(bank[:, 0:nn], wip[:, m % 2, k, :], xlnT[:, k, o0:o0 + nn])
                 for k in range(KH)], waits=w, mark=f"ipm_{m}_{nb}")
            dve(lambda e, bank=bank, m=m, o0=o0, nn=nn:
                e.tensor_copy(hbc[:, m % 3, o0:o0 + nn], bank[:, 0:nn]),
                waits=[("pe", f"ipm_{m}_{nb}")], mark=f"ipe_{m}_{nb}")
        MK(f"ipu_{m}", "pe")
        for j in range(TT):
            half = j // 4
            bank = pb[2 + half]
            reg = bank[:, 128 * (j % 4):128 * (j % 4) + 128]
            w = [("dve", f"ipe_{m}_2"), ("dw", f"cdt_{m}"), ("dw", "smallw_ld"),
                 ("dw", "convb_ld")]
            if m >= 1 and j % 4 == 0:
                w.append(("act", f"cve_{m - 1}_{half}"))
            lst = [(reg, hbc[:, m % 3, 128 * j + k:128 * j + k + 128], cdt[:, m % 2, k, :])
                   for k in range(4)]
            lst.append((reg, ones_b[:, :], convb_bf[0:1, 128 * m:128 * (m + 1)]))
            mmg(lst, waits=w, mark=f"cvm_{m}_{j}")
        MK(f"cvu_{m}", "pe")
        for half in range(2):
            act(lambda e, m=m, half=half:
                e.activation(convo[:, 4 * half:4 * half + 4, 128 * m:128 * (m + 1)],
                             pb[2 + half][:, 0:512], AF.Silu),
                waits=[("pe", f"cvm_{m}_{4 * half + 3}")], mark=f"cve_{m}_{half}")
    MK("conv_done", "act")
    if debug:
        dma("sync", t_convo.ap().rearrange("(i p) d -> p i d", p=128), convo[:, :, :],
            waits=[("act", "conv_done")])

    for j in range(TT):
        reg = pb[4][:, 32 * j:32 * (j + 1)]
        mmg([(reg, xlnT[:, k, 3 + 128 * j:3 + 128 * j + 128], wdt[:, k, :])
             for k in range(KH)],
            waits=[("dx", "xlnT"), ("dw", "smallw_ld")], mark=f"dtm_{j}")
    dve(lambda e: e.tensor_copy(dtraw[:, :, :],
                                pb[4][:, 0:256].rearrange("p (a b) -> p a b", a=TT)),
        waits=[("pe", "dtm_7")], mark="dtraw_ev")

    dtb_bc = bass.AP(dtb, 0, [[HEADS, 128], [0, TT], [1, HEADS]])
    dve(lambda e: e.tensor_tensor(dtv[:, :, :], dtraw[:, :, :], dtb_bc, ALU.add),
        waits=[("dve", "dtraw_ev"), ("dx", "consts_ld")], mark="dtz")
    act(lambda e: e.activation(sp1[:, :, :], dtv[:, :, :], AF.Abs),
        waits=[("dve", "dtz")], mark="sp_abs")
    act(lambda e: e.activation(dtraw[:, :, :], sp1[:, :, :], AF.Exp, scale=-1.0),
        mark="sp_exp")
    act(lambda e: e.activation(eaT[:, :, :], dtraw[:, :, :], AF.Ln, bias=1.0),
        mark="sp_ln")
    dve(lambda e: e.tensor_scalar(sp1[:, :, :], dtv[:, :, :], 0.0, None, ALU.max),
        waits=[("act", "sp_exp")], mark="spm")
    dve(lambda e: e.tensor_tensor(dtv[:, :, :], sp1[:, :, :], eaT[:, :, :], ALU.add),
        waits=[("act", "sp_ln"), ("dve", "spm")], mark="spa")
    dve(lambda e: e.tensor_scalar(dtv[:, :, :], dtv[:, :, :], 100.0, None, ALU.min),
        waits=[("dve", "spa")], mark="dt_done")
    for j in range(TT):
        w = [("dve", "dt_done")]
        if j > 0:
            w.append(("dve", f"dtfm_{j - 1}"))
        tpose(pF[0:32, 0:128], dtv[:, j, :], idmf[:, :], waits=w, mark=f"dtT_{j}")
        dve(lambda e, j=j: e.tensor_copy(dt_fm[:, 128 * j:128 * (j + 1)], pF[0:32, 0:128]),
            waits=[("pe", f"dtT_{j}"), ("dw", "convb_ld")], mark=f"dtfm_{j}")
    act(lambda e: e.activation(a_e[:, :], alog[:, :], AF.Exp),
        waits=[("dx", "consts_ld")], mark="a_e")
    dve(lambda e: e.tensor_scalar(ad_fm[:, :], dt_fm[:, :], a_e[:, :], -1.0,
                                  ALU.mult, ALU.mult),
        waits=[("dve", "dtfm_7"), ("act", "a_e")], mark="ad")
    dve(lambda e: e.memset(zeros32[:, :], 0.0), mark="z32")
    for c in range(NCH):
        dve(lambda e, c=c: e.tensor_tensor_scan(
            ac_fm[:, CH * c:CH * (c + 1)], ad_fm[:, CH * c:CH * (c + 1)],
            zeros32[:, :], 0.0, ALU.add, ALU.add),
            waits=[("dve", "ad"), ("dve", "z32")] if c == 0 else ())
    MK("acum", "dve")
    dve(lambda e: e.tensor_copy(ac_hi[:, :], ac_fm[:, :]),
        waits=[("dve", "acum")], mark="achi")
    dve(lambda e: e.tensor_tensor(ac_lo[:, :], ac_fm[:, :], ac_hi[:, :], ALU.subtract),
        waits=[("dve", "achi")], mark="achl")
    for c in range(NCH):
        dve(lambda e, c=c: e.tensor_scalar(
            ad_fm[:, CH * c:CH * (c + 1)], ac_fm[:, CH * c:CH * (c + 1)],
            ac_fm[:, CH * c + 255:CH * c + 256], -1.0, ALU.subtract, ALU.mult),
            waits=[("dve", "achl")])
    MK("dcy_pre", "dve")
    act(lambda e: e.activation(dcy_fm[:, :], ad_fm[:, :], AF.Exp),
        waits=[("dve", "dcy_pre")], mark="dcy")
    for j in range(TT):
        w = [("dve", "acum"), ("dve", f"dtfm_{TT - 1}")]
        if j > 0:
            w.append(("dve", f"acT_{j - 1}"))
        tpose(pF[0:128, 0:32], ac_fm[:, 128 * j:128 * (j + 1)], idmf[0:32, 0:32],
              waits=w, mark=f"acTp_{j}")
        dve(lambda e, j=j: e.tensor_copy(acT[:, j, :], pF[0:128, 0:32]),
            waits=[("pe", f"acTp_{j}")], mark=f"acT_{j}")
    for j in range(TT):
        w = [("act", "dcy"), ("dve", "acT_7" if j == 0 else f"dT_{j - 1}")]
        tpose(pF[0:128, 0:32], dcy_fm[:, 128 * j:128 * (j + 1)], idmf[0:32, 0:32],
              waits=w, mark=f"dTp_{j}")
        dve(lambda e, j=j: e.tensor_copy(dT[:, j, :], pF[0:128, 0:32]),
            waits=[("pe", f"dTp_{j}")], mark=f"dT_{j}")
    act(lambda e: e.activation(eaT[:, :, :], acT[:, :, :], AF.Exp),
        waits=[("dve", "acT_7")], mark="eaT")
    for c in range(NCH):
        w = [("dve", "dT_7" if c == 0 else f"srow_{c - 1}")]
        if c > 0:
            w.append(("act", f"rbc_{c - 1}"))
        tpose(pF[0:1, 0:32], ac_fm[:, CH * c + 255:CH * c + 256], idmf[0:32, 0:32],
              waits=w, mark=f"srT_{c}")
        dve(lambda e: e.tensor_copy(srow[:, :], pF[0:1, 0:32]),
            waits=[("pe", f"srT_{c}")], mark=f"srow_{c}")
        mmg([(pF[0:128, 0:32], ones_f[:, :], srow[:, :])],
            waits=[("dve", f"srow_{c}")], mark=f"rbm_{c}")
        act(lambda e, c=c: e.activation(r_bc[:, c, :], pF[0:128, 0:32], AF.Exp),
            waits=[("pe", f"rbm_{c}")], mark=f"rbc_{c}")
    MK("scanprep", "act")

    if debug:
        dma("sync", t_dt.ap(), dtv[:, :, :].rearrange("p a b -> p (a b)"),
            waits=[("dve", "dt_done")])
        dma("sync", t_ac.ap(), ac_fm[:, :], waits=[("dve", "achl")])
    for j in range(TT):
        w = [("act", "conv_done")]
        if j > 0:
            w.append(("dve", f"bfm_{j - 1}"))
        tpose(pT[:, 0:128], convo[:, j, 4096:4224], idmb[:, :], waits=w, mark=f"bTp_{j}")
        dve(lambda e, j=j: e.tensor_copy(b_fm[:, j, :], pT[:, 0:128]),
            waits=[("pe", f"bTp_{j}")], mark=f"bfm_{j}")
    for j in range(TT):
        w = [("dve", "bfm_7" if j == 0 else f"cfm_{j - 1}")]
        tpose(pT[:, 0:128], convo[:, j, 4224:4352], idmb[:, :], waits=w, mark=f"cTp_{j}")
        dve(lambda e, j=j: e.tensor_copy(c_fm[:, j, :], pT[:, 0:128]),
            waits=[("pe", f"cTp_{j}")], mark=f"cfm_{j}")
    dve(lambda e: e.memset(S[:, :], 0.0))
    dve(lambda e: e.memset(S_bf[:, :], 0.0), mark="S0")

    def evac_head(c, h):
        j0 = 2 * c
        Yb = pb[3 + h % 2]
        for jj in (0, 1):
            w = [("pe", f"y_{c}_{h}")] if jj == 0 else []
            if jj == 0 and h == 0:
                w += [("dx", f"ytsp_{c - 1}")] if c > 0 else [("dx", "xlnT")]
            dve(lambda e, Yb=Yb, jj=jj, j0=j0, h=h:
                e.scalar_tensor_tensor(ytot[:, jj, 128 * h:128 * (h + 1)],
                                       convo[:, j0 + jj, 128 * h:128 * (h + 1)],
                                       ddv[:, h:h + 1], Yb[:, 128 * jj:128 * (jj + 1)],
                                       ALU.mult, ALU.add),
                waits=w)
        MK(f"yt_{c}_{h}", "dve")
        dve(lambda e, c=c, h=h:
            e.scalar_tensor_tensor(S[:, 128 * h:128 * (h + 1)],
                                   S[:, 128 * h:128 * (h + 1)], r_bc[:, c, h:h + 1],
                                   pF[:, 0:128], ALU.mult, ALU.add),
            waits=[("pe", f"st_{c}_{h}"), ("act", "scanprep")], mark=f"supd_{c}_{h}")
        dve(lambda e, h=h: e.tensor_copy(S_bf[:, 128 * h:128 * (h + 1)],
                                         S[:, 128 * h:128 * (h + 1)]),
            waits=[("pe", f"y_{c}_{h}"), ("dve", f"supd_{c}_{h}")], mark=f"sbf_{c}_{h}")

    for c in range(NCH):
        j0, j1 = 2 * c, 2 * c + 1
        for q in range(4):
            nh = 4 * c + q
            gw = gwa if nh % 2 == 0 else gwb
            if nh >= 2:
                w = [("pe", f"gau_{nh - 2}")]
            elif nh == 0:
                w = [("dve", "ad"), ("act", "dcy")]
            else:
                w = [("pe", f"cvu_{NCT - 1}")]
            dma("gp", gw[:, :, :],
                i_wT.ap()[:, 256 * nh:256 * (nh + 1)].rearrange("(k p) c -> p k c", p=128),
                waits=w, mark=f"gaw_{nh}")
        crhs = c_fm[:, j0:j0 + 2, :].rearrange("p a b -> p (a b)")
        w = [("dve", "cfm_7"), ("dve", "S0")]
        if c > 0:
            w.append(("dve", f"mt_{c - 1}_31"))
        mmg([(pb[0][:, 0:256], b_fm[:, j0, :], crhs)], waits=w)
        mmg([(pb[0][:, 256:512], b_fm[:, j1, :], crhs)], mark=f"gt_{c}")

        def emit_R(h, c=c):
            bank = pb[1 + h % 2]
            w = [("dve", "achl"), ("dw", "smallw_ld")]
            if h >= 2:
                w += [("dve", f"argm_{c}_{h - 2}"), ("act", f"er_{c}_{h - 2}")]
            elif c > 0:
                w += [("dve", f"argm_{c - 1}_{30 + h}"), ("act", f"er_{c - 1}_{30 + h}")]
            mmg([(bank[:, 0:256], selb[:, 128 * h:128 * (h + 1)],
                  ac_hi[:, CH * c:CH * (c + 1)]),
                 (bank[:, 0:256], selb[:, 128 * h:128 * (h + 1)],
                  ac_lo[:, CH * c:CH * (c + 1)])],
                waits=w, mark=f"R_{c}_{h}")

        emit_R(0)
        emit_R(1)
        for h in range(HEADS):
            pp = h % 2
            Rb = pb[1 + h % 2]
            g = 32 * c + h
            if h > 0:
                evac_head(c, h - 1)
            w = [("pe", f"R_{c}_{h}")]
            if h > 0:
                w.append(("act", f"E_{c}_{h - 1}"))
            elif c > 0:
                w.append(("act", f"E_{c - 1}_31"))
            dve(lambda e, Rb=Rb, j0=j0, h=h:
                e.scalar_tensor_tensor(argm[:, 0:256], Rb[:, 0:256],
                                       acT[:, j0, h:h + 1], dcapa[:, :],
                                       ALU.subtract, ALU.min),
                waits=w)
            dve(lambda e, Rb=Rb, j1=j1, h=h:
                e.scalar_tensor_tensor(argm[:, 256:384], Rb[:, 128:256],
                                       acT[:, j1, h:h + 1], dcapb[:, :],
                                       ALU.subtract, ALU.min),
                mark=f"argm_{c}_{h}")
            w = [("act", f"E_{c}_{h}")]
            if h > 0:
                w.append(("pe", f"y_{c}_{h - 1}"))
            elif c > 0:
                w.append(("pe", f"y_{c - 1}_31"))
            dve(lambda e: e.tensor_tensor(mt_sb[:, 0:256], e_sb[:, 0:256],
                                          pb[0][:, 0:256], ALU.mult),
                waits=w)
            dve(lambda e: e.tensor_tensor(mt_sb[:, 256:384], e_sb[:, 256:384],
                                          pb[0][:, 384:512], ALU.mult))
            dve(lambda e, j0=j0, c=c, h=h:
                e.tensor_tensor(ceb[:, :],
                                c_fm[:, j0:j0 + 2, :].rearrange("p a b -> p (a b)"),
                                erow[:, :], ALU.mult),
                waits=[("act", f"er_{c}_{h}")], mark=f"mt_{c}_{h}")
            w = [("dve", "dt_done"), ("dve", "dT_7")]
            if h >= 2:
                w.append(("pe", f"st_{c}_{h - 2}"))
            elif c > 0:
                w.append(("pe", f"st_{c - 1}_{30 + h}"))
            for jj in (0, 1):
                act(lambda e, pp=pp, jj=jj, j=j0 + jj, h=h:
                    e.activation(xdt[:, pp, jj, :], convo[:, j, 128 * h:128 * (h + 1)],
                                 AF.Copy, scale=dtv[:, j, h:h + 1]),
                    waits=w if jj == 0 else ())
            for jj in (0, 1):
                act(lambda e, pp=pp, jj=jj, j=j0 + jj, h=h:
                    e.activation(bdb[:, pp, jj, :], convo[:, j, 4096:4224],
                                 AF.Copy, scale=dT[:, j, h:h + 1]))
            MK(f"xb_{c}_{h}", "act")
            w = [("dve", f"argm_{c}_{h}")]
            if h > 0:
                w.append(("dve", f"mt_{c}_{h - 1}"))
            elif c > 0:
                w.append(("dve", f"mt_{c - 1}_31"))
            act(lambda e: e.activation(e_sb[:, 0:384], argm[:, 0:384], AF.Exp),
                waits=w, mark=f"E_{c}_{h}")
            act(lambda e, Rb=Rb: e.activation(erow[:, :], Rb[:, 0:256], AF.Exp),
                mark=f"er_{c}_{h}")
            if g > 0:
                gp, jp, np_ = g - 1, (g - 1) % 8, (g - 1) // 8
                w = [("pe", f"gam_{gp}")]
                if gp >= 2:
                    w.append(("dx", f"sgw_{gp - 2}"))
                act(lambda e, gp=gp: e.activation(sgblk[:, gp % 2, :], pb[5][:, 0:256],
                                                  AF.Copy),
                    waits=w, mark=f"sge_{gp}")
                dma("sync", h_sg.ap()[128 * jp:128 * (jp + 1),
                                      256 * np_:256 * (np_ + 1)],
                    sgblk[:, gp % 2, :], waits=[("act", f"sge_{gp}")], mark=f"sgw_{gp}")
            Yb = pb[3 + h % 2]
            w = [("dve", f"mt_{c}_{h}"), ("act", f"xb_{c}_{h}")]
            if h >= 2:
                w.append(("dve", f"yt_{c}_{h - 2}"))
            elif c > 0:
                w.append(("dve", f"yt_{c - 1}_{30 + h}"))
            else:
                w += [("act", "scanprep"), ("dve", f"ipe_{NCT - 1}_2"),
                      ("act", "conv_done"), ("dve", "dtraw_ev")]
            if c > 0:
                w.append(("dve", f"sbf_{c - 1}_{h}"))
            mmg([(Yb[:, 0:128], mt_sb[:, 0:128], xdt[:, pp, 0, :]),
                 (Yb[:, 0:128], ceb[:, 0:128], S_bf[:, 128 * h:128 * (h + 1)])],
                waits=w)
            mmg([(Yb[:, 128:256], mt_sb[:, 128:256], xdt[:, pp, 0, :]),
                 (Yb[:, 128:256], mt_sb[:, 256:384], xdt[:, pp, 1, :]),
                 (Yb[:, 128:256], ceb[:, 128:256], S_bf[:, 128 * h:128 * (h + 1)])],
                mark=f"y_{c}_{h}")
            w = []
            if h >= 1:
                w.append(("dve", f"supd_{c}_{h - 1}"))
            elif c > 0:
                w.append(("dve", f"supd_{c - 1}_31"))
            else:
                w.append(("act", "scanprep"))
            mmg([(pF[:, 0:128], bdb[:, pp, 0, :], xdt[:, pp, 0, :]),
                 (pF[:, 0:128], bdb[:, pp, 1, :], xdt[:, pp, 1, :])],
                waits=w, mark=f"st_{c}_{h}")
            nh, jg = g // 8, g % 8
            gw = gwa if nh % 2 == 0 else gwb
            w = [("dw", f"gaw_{nh}")]
            if g > 0:
                w.append(("act", f"sge_{g - 1}"))
            else:
                w += [("dve", f"ipe_{NCT - 1}_2"), ("act", "conv_done"),
                      ("dve", "dtraw_ev")]
            mmg([(pb[5][:, 0:256], xlnT[:, k, 3 + 128 * jg:3 + 128 * jg + 128],
                  gw[:, k, :]) for k in range(KH)], waits=w, mark=f"gam_{g}")
            if jg == 7:
                MK(f"gau_{nh}", "pe")
            if h + 2 < HEADS:
                emit_R(h + 2)
        evac_head(c, HEADS - 1)
        MK(f"chunk_{c}", "dve")
        dma("sync", h_ytot.ap()[CH * c:CH * (c + 1), :].rearrange("(i p) d -> p i d", p=128),
            ytot[:, :, :], waits=[("dve", f"chunk_{c}")], mark=f"ytsp_{c}")
    act(lambda e: e.activation(sgblk[:, 127 % 2, :], pb[5][:, 0:256], AF.Copy),
        waits=[("pe", "gam_127"), ("dx", "sgw_125")], mark="sge_127")
    dma("sync", h_sg.ap()[128 * 7:128 * 8, 256 * 15:256 * 16], sgblk[:, 1, :],
        waits=[("act", "sge_127")], mark="sgw_127")
    MKQ("sg_done", "sync")
    dve(lambda e: e.tensor_copy(c_fm2[:, :, :].rearrange("p a b -> p (a b)"),
                                c_fm[:, :, :].rearrange("p a b -> p (a b)")),
        waits=[("dve", "mt_3_31"), ("pe", "gau_15")], mark="cfm2")

    dve(lambda e: e.tensor_copy(S_bf[:, :], S[:, :]),
        waits=[("dve", "supd_3_31")], mark="sout_rdy")
    dma("gp", h_sout.ap(), S_bf[:, :], waits=[("dve", "sout_rdy")], mark="sout_sp")
    if debug:
        dma("sync", t_sout.ap(), S_bf[:, :], waits=[("dve", "sout_rdy")])
    OP("gp", lambda e: e.collective_compute(
        "AllGather", ALU.bypass, replica_groups=[list(range(8))],
        ins=[h_sout.ap().opt()], outs=[h_sgat.ap().opt()]),
       inc=("cc", 1), waits=[("dw", "sout_sp")])
    ccv = cnt["cc"]
    OP("gp", lambda e, v=ccv: e.wait_ge(sh["cc"], v), inc=None)
    for r in range(8):
        w = [("dve", f"sina_{r - 2}")] if r >= 2 else []
        dma("gp", sgat8[:, r % 2, :], h_sgat.ap()[128 * r:128 * (r + 1), :], waits=w,
            mark=f"sgat_{r}")
    dma("gp", nw_bf[:, :], i_nw.ap(), mark="nw_ld")

    for r in range(8):
        if r == 0:
            dve(lambda e: e.tensor_scalar(S_in[:, :], sgat8[:, 0, :], maskv[:, 0:1],
                                          None, ALU.mult),
                waits=[("dw", "sgat_0"), ("dw", "sout_sp"), ("dx", "consts_ld")],
                mark="sina_0")
        else:
            dve(lambda e, r=r: e.scalar_tensor_tensor(S_in[:, :], sgat8[:, r % 2, :],
                                                      maskv[:, r:r + 1], S_in[:, :],
                                                      ALU.mult, ALU.add),
                waits=[("dw", f"sgat_{r}"), ("dve", f"sina_{r - 1}")], mark=f"sina_{r}")
    MK("sin_done", "dve")
    dve(lambda e: e.memset(q_bc[:, 0, :], 1.0),
        waits=[("act", "scanprep"), ("pe", "gau_15")], mark="qb_0")
    for c in range(1, NCH):
        dve(lambda e, c=c: e.tensor_tensor(q_bc[:, c, :], q_bc[:, c - 1, :],
                                           r_bc[:, c - 1, :], ALU.mult),
            waits=[("dve", f"qb_{c - 1}")], mark=f"qb_{c}")
    MK("qbc", "dve")

    def corr_phase(c, pend=None):
        cpp = c % 2
        j0 = 2 * c
        w = [("dx", f"ytsp_{c}")]
        if c >= 2:
            w.append(("dve", f"ycons_{c - 2}"))
        dma("sync", ysb[:, cpp, :, :],
            h_ytot.ap()[CH * c:CH * (c + 1), :].rearrange("(i p) d -> p i d", p=128),
            waits=w, mark=f"yld_{c}")
        w = [("dx", "sg_done"), ("dve", "chunk_3"), ("dx", "ytsp_3"),
             ("pe", "gau_15"), ("dve", "cfm2")]
        if c >= 2:
            w.append(("dve", f"sgcons_{c - 2}"))
        dma("sync", sgc[:, cpp, :, :],
            h_sg.ap()[CH * c:CH * (c + 1), :].rearrange("(i p) d -> p i d", p=128),
            waits=w, mark=f"sgld_{c}")
        for jj in (0, 1):
            w = [("dve", "qbc"), ("act", "eaT")]
            if c >= 2 and jj == 0:
                w.append(("act", f"cse_{c - 2}_31"))
            dve(lambda e, c=c, jj=jj, j0=j0, cpp=cpp: e.tensor_tensor(
                ecq[:, cpp, jj, :], eaT[:, j0 + jj, :], q_bc[:, c, :], ALU.mult),
                waits=w)
        MK(f"ecq_{c}", "dve")
        for h in range(HEADS):
            bank = pb[h % 2]
            pp2 = h % 2
            w = [("dve", "sin_done"), ("dve", "cfm2")]
            if h >= 2:
                w.append(("act", f"cse_{c}_{h - 2}"))
            elif c > 0:
                w.append(("act", f"cse_{c - 1}_{30 + h}"))
            mmg([(bank[:, 0:128], c_fm2[:, j0, :], S_in[:, 128 * h:128 * (h + 1)])],
                waits=w)
            mmg([(bank[:, 128:256], c_fm2[:, j0 + 1, :], S_in[:, 128 * h:128 * (h + 1)])],
                mark=f"corr_{c}_{h}")
            for jj in (0, 1):
                act(lambda e, bank=bank, pp2=pp2, jj=jj, h=h:
                    e.activation(ysc2[:, pp2, 128 * jj:128 * (jj + 1)],
                                 bank[:, 128 * jj:128 * (jj + 1)], AF.Copy,
                                 scale=ecq[:, cpp, jj, h:h + 1]),
                    waits=([("pe", f"corr_{c}_{h}"), ("dve", f"ecq_{c}"),
                            ("pe", "gau_15")]
                           + ([("dve", f"cadd_{c}_{h - 1}")] if h > 0 else
                              ([("dve", f"cadd_{c - 1}_31")] if c > 0 else []))
                           if jj == 0 else ()))
            MK(f"cse_{c}_{h}", "act")
            for jj in (0, 1):
                dve(lambda e, cpp=cpp, pp2=pp2, jj=jj, h=h:
                    e.tensor_tensor(ysb[:, cpp, jj, 128 * h:128 * (h + 1)],
                                    ysb[:, cpp, jj, 128 * h:128 * (h + 1)],
                                    ysc2[:, pp2, 128 * jj:128 * (jj + 1)], ALU.add),
                    waits=([("act", f"cse_{c}_{h}"), ("dx", f"yld_{c}")]
                           if jj == 0 else ()))
            MK(f"cadd_{c}_{h}", "dve")
            if pend:
                pend.pop(0)()
                if h == HEADS - 1:
                    while pend:
                        pend.pop(0)()

    def gate_phase(c, defer=False):
        thunks = []

        def T(fn):
            if defer:
                thunks.append(fn)
            else:
                fn()
        cpp = c % 2
        for jj in (0, 1):
            w = [("dx", f"sgld_{c}")]
            if c > 0:
                w.append(("dve", f"y3_{c - 1}_{jj}"))
            T(lambda jj=jj, w=w: act(
                lambda e, cpp=cpp, jj=jj: e.activation(y2[:, jj, :],
                                                       sgc[:, cpp, jj, :], AF.Silu),
                waits=w, mark=f"silu_{c}_{jj}"))
        for jj in (0, 1):
            T(lambda jj=jj: dve(lambda e, cpp=cpp, jj=jj: e.tensor_tensor(
                y2[:, jj, :], y2[:, jj, :], ysb[:, cpp, jj, :], ALU.mult),
                waits=[("dve", f"cadd_{c}_31"), ("act", f"silu_{c}_{jj}"),
                       ("dx", f"yld_{c}")], mark=f"y2_{c}_{jj}"))
            T(lambda jj=jj: act(
                lambda e, jj=jj: e.activation(junkb[:, :], y2[:, jj, :], AF.Square,
                                              accum_out=ysq[:, :]),
                waits=[("dve", f"y2_{c}_{jj}")], mark=f"ysq_{c}_{jj}"))
            T(lambda jj=jj: dve(
                lambda e: e.tensor_scalar(rr2[:, :], ysq[:, :], 1.0 / INTER, EPS_RMS,
                                          ALU.mult, ALU.add),
                waits=[("act", f"ysq_{c}_{jj}")], mark=f"rmsv_{c}_{jj}"))
            T(lambda jj=jj: act(lambda e: e.activation(rr2[:, :], rr2[:, :], AF.Ln),
                                waits=[("dve", f"rmsv_{c}_{jj}")]))
            T(lambda jj=jj: act(lambda e: e.activation(rr2[:, :], rr2[:, :], AF.Exp,
                                                       scale=-0.5),
                                mark=f"rmsr_{c}_{jj}"))
            T(lambda jj=jj: dve(
                lambda e, jj=jj: e.tensor_scalar(y2[:, jj, :], y2[:, jj, :], rr2[:, :],
                                                 None, ALU.mult),
                waits=[("act", f"rmsr_{c}_{jj}")], mark=f"y2s_{c}_{jj}"))
            w = [("dw", "nw_ld")]
            if c > 0 or jj > 0:
                w.append(("dx", f"y3w_{c}_{jj - 1}" if jj else f"y3w_{c - 1}_1"))
            T(lambda jj=jj, w=w: dve(
                lambda e, jj=jj: e.tensor_tensor(y3t[:, jj, :], y2[:, jj, :],
                                                 nw_bf[:, :], ALU.mult),
                waits=w + [("dve", f"y2s_{c}_{jj}")], mark=f"y3_{c}_{jj}"))
            T(lambda jj=jj: dma(
                "sync", h_y3.ap()[CH * c + 128 * jj:CH * c + 128 * (jj + 1), :],
                y3t[:, jj, :], waits=[("dve", f"y3_{c}_{jj}")], mark=f"y3w_{c}_{jj}"))
        T(lambda: MK(f"ycons_{c}", "dve"))
        T(lambda: MK(f"sgcons_{c}", "dve"))
        return thunks

    corr_phase(0)
    corr_phase(1, pend=gate_phase(0, defer=True))
    corr_phase(2, pend=gate_phase(1, defer=True))
    corr_phase(3, pend=gate_phase(2, defer=True))
    gate_phase(3)
    MKQ("pass2_done", "sync")

    for kk in range(32):
        dma("sync", yT[:, kk, :], h_y3.ap()[:, 128 * kk:128 * (kk + 1)],
            transpose=True, waits=[("dx", "pass2_done")], mark=f"yTp_{kk}")
    it = 0
    for nb in range(4):
        for grp in range(2):
            for kg in range(4):
                wo = wo_a if it % 2 == 0 else wo_b
                w = [("pe", f"wou_{it - 2}")] if it >= 2 else [("dx", "pass2_done")]
                dma("gp", wo[:, :, :],
                    i_woT.ap()[1024 * kg:1024 * (kg + 1), 512 * nb:512 * (nb + 1)]
                    .rearrange("(k p) c -> p k c", p=128),
                    waits=w, mark=f"wo_{it}")
                for j4 in range(4):
                    j = grp * 4 + j4
                    bank = pb[j4]
                    w2 = [("dw", f"wo_{it}")] + [(None, f"yTp_{8 * kg + q8}")
                                                 for q8 in range(8)]
                    if kg == 0 and (nb > 0 or grp > 0):
                        w2.append(("dve", f"aoe_{nb * 2 + grp - 1}_{j4}"))
                    for k8 in range(8):
                        kf = 8 * kg + k8
                        OP("pe", lambda e, bank=bank, kf=kf, j=j, wo=wo, k8=k8:
                           e.matmul(bank[:, 0:512], yT[:, kf, 128 * j:128 * (j + 1)],
                                    wo[:, k8, :], start=(kf == 0), stop=(kf == 31)),
                           inc=("pe", 1) if k8 == 7 else None,
                           waits=w2 if k8 == 0 else ())
                    MK(f"wog_{it}_{j4}", "pe")
                MK(f"wou_{it}", "pe")
                it += 1
            for j4 in range(4):
                j = grp * 4 + j4
                dve(lambda e, j=j, j4=j4, nb=nb:
                    e.tensor_copy(ao[:, j, 512 * nb:512 * (nb + 1)], pb[j4][:, 0:512]),
                    waits=[("pe", f"wog_{it - 1}_{j4}")], mark=f"aoe_{nb * 2 + grp}_{j4}")
    MK("ao_done", "dve")
    if debug:
        dma("sync", t_ao.ap().rearrange("(i p) d -> p i d", p=128), ao[:, :, :],
            waits=[("dve", "ao_done")], mark="t_ao_dma")
    dma("gp", wrb[:, :, :], i_wr.ap().rearrange("(k p) c -> p k c", p=128),
        waits=[("pe", "gau_15")], mark="wr_ld")
    def rt_tile(j):
        bank = pb[4 + (j % 2)]
        w = [("dw", "wr_ld"), ("act", f"x2e_{j}")]
        if j >= 2:
            w.append(("act", f"pre_{j - 2}"))
        mmg([(bank[:, 0:128], xlnT2[:, k, 128 * j:128 * (j + 1)], wrb[:, k, :])
             for k in range(KH)], waits=w, mark=f"rtm_{j}")
        dve(lambda e, bank=bank: e.reduce_max(rtc[:, :], bank[:, 0:128], axis=AX.X,
                                              negate=True),
            waits=[("pe", f"rtm_{j}")] + ([("dve", "mix_done")] if j == 0 else []),
            mark=f"rmx_{j}")
        act(lambda e, bank=bank, j=j: e.activation(probs[:, j, :], bank[:, 0:128],
                                                   AF.Exp, bias=rtc[:, :],
                                                   accum_out=s4c[:, :]),
            waits=[("dve", f"rmx_{j}")], mark=f"pre_{j}")
        dve(lambda e: e.reciprocal(rtc[:, :], s4c[:, :]),
            waits=[("act", f"pre_{j}")], mark=f"rc1_{j}")
        dve(lambda e, j=j: e.tensor_scalar(probs[:, j, :], probs[:, j, :], rtc[:, :],
                                           None, ALU.mult),
            waits=[("dve", f"rc1_{j}")], mark=f"pn_{j}")
        dve(lambda e, j=j: e.max(out=w8[:, :], in_=probs[:, j, :]),
            waits=[("dve", f"pn_{j}")], mark=f"mx8_{j}")
        dve(lambda e: e.reduce_sum(s4c[:, :], w8[:, 0:4], axis=AX.X),
            waits=[("dve", f"mx8_{j}")], mark=f"rs_{j}")
        dve(lambda e: e.tensor_scalar(s4c[:, :], s4c[:, :], EPS_MOE, None, ALU.add),
            waits=[("dve", f"rs_{j}")], mark=f"s4a_{j}")
        dve(lambda e: e.reciprocal(rtc[:, :], s4c[:, :]),
            waits=[("dve", f"s4a_{j}")], mark=f"rc2_{j}")
        dve(lambda e, j=j: e.tensor_scalar(mskt[:, :], probs[:, j, :], w8[:, 3:4],
                                           None, ALU.is_ge),
            waits=[("dve", f"mx8_{j}")], mark=f"mk_{j}")
        dve(lambda e, j=j: e.scalar_tensor_tensor(ctm[:, j, :], probs[:, j, :],
                                                  rtc[:, :], mskt[:, :],
                                                  ALU.mult, ALU.mult),
            waits=[("dve", f"rc2_{j}"), ("dve", f"mk_{j}")], mark=f"ctm_{j}")
        w = [("dve", f"ctm_{j}")]
        if j > 0:
            w.append(("dve", f"cfm2_{j - 1}"))
        tpose(pT[:, 0:128], ctm[:, j, :], idmb[:, :], waits=w, mark=f"cT2_{j}")
        dve(lambda e, j=j: e.tensor_copy(cfm[:, j, :], pT[:, 0:128]),
            waits=[("pe", f"cT2_{j}")], mark=f"cfm2_{j}")
        mmg([(pb[3][:, 8 * j:8 * (j + 1)], cfm[:, j, :], mixs[:, :])],
            waits=[("dve", f"cfm2_{j}"), ("dve", "mix_done")], mark=f"cbm_{j}")
    for j in range(TT):
        w = [("pe", "wou_31")]
        if j >= 2:
            w.append(("dve", f"x1_{j - 2}"))
        dma("sync", xr[:, j % 2, :], i_x.ap()[128 * j:128 * (j + 1), :], waits=w,
            mark=f"xr_{j}")
        fw = [("dve", "ao_done")] + ([("act", f"xl2_{j - 2}")] if j >= 2 else [])
        fwa = list(fw) + ([("dx", f"xl2w_{j - 2}"), ("pe", f"x2p_{j - 2}")]
                          if j >= 2 else [])
        ln_stats(ao[:, j, :], xl2t[:, j % 2, :], fw, fwa, f"l2_{j}", cs=j % 2)
        rc8, bc8 = COLS[j % 2][4], COLS[j % 2][5]
        if debug:
            dve(lambda e, j=j: e.tensor_copy(dcols[:, j, 0:1], mcol[:, :]),
                waits=[("dve", f"l2_{j}b")])
            dve(lambda e, j=j: e.tensor_copy(dcols[:, j, 1:2], vcol[:, :]))
            dve(lambda e, j=j: e.tensor_copy(dcols[:, j, 2:3], ucol[:, :]))
            dve(lambda e, j=j: e.tensor_copy(dcols[:, j, 3:4], rcol[:, :]))
            dve(lambda e, j=j: e.tensor_copy(dcols[:, j, 4:5], bcol[:, :]))
            dve(lambda e, j=j: e.tensor_copy(dcols[:, j, 5:6], scol[:, :]))
        dve(lambda e, j=j, bc8=bc8: e.tensor_scalar(x1t[:, j % 2, :], xr[:, j % 2, :],
                                                    bc8[:, :], None, ALU.add),
            waits=[("dx", f"xr_{j}"), ("dve", f"l2_{j}b")]
                  + ([("dx", f"x1w_{j - 2}"), ("act", f"xl2_{j - 2}")] if j >= 2 else []),
            mark=f"xoff_{j}")
        dve(lambda e, j=j, rc8=rc8: e.scalar_tensor_tensor(x1t[:, j % 2, :], ao[:, j, :],
                                                           rc8[:, :], x1t[:, j % 2, :],
                                                           ALU.mult, ALU.add),
            waits=[("dve", f"xoff_{j}")], mark=f"x1_{j}")
        dma("sync", h_x1.ap()[128 * j:128 * (j + 1), :], x1t[:, j % 2, :],
            waits=[("dve", f"x1_{j}")], mark=f"x1w_{j}")
        if debug and j < 2:
            dma("sync", t_xr.ap()[:, HID * j:HID * (j + 1)], xr[:, j % 2, :],
                waits=[("dve", f"x1_{j}")])
            dma("sync", t_x1t.ap()[:, HID * j:HID * (j + 1)], x1t[:, j % 2, :],
                waits=[("dve", f"x1_{j}")])
        fw2 = [("dve", f"x1_{j}")]
        fwa2 = list(fw2) + ([("dx", f"xl2w_{j - 2}")] if j >= 2 else [])
        ln_stats(x1t[:, j % 2, :], xl2t[:, j % 2, :], fw2, fwa2, f"l2s_{j}", cs=j % 2)
        aw = [("dve", f"l2s_{j}b")]
        if j >= 2:
            aw.append(("dx", f"xl2w_{j - 2}"))
        act(lambda e, j=j, rc8=rc8, bc8=bc8:
            e.activation(xl2t[:, j % 2, :], x1t[:, j % 2, :], AF.Identity,
                         bias=bc8[:, :], scale=rc8[:, :]),
            waits=aw, mark=f"xl2_{j}")
        dma("gp", h_xln2.ap()[128 * j:128 * (j + 1), :], xl2t[:, j % 2, :],
            waits=[("act", f"xl2_{j}")], mark=f"xl2w_{j}")
        for kk in range(KH):
            bank = pb[kk % 3]
            w = [("act", f"xl2_{j}")]
            if j == 0 and kk < 3:
                w.append(("dve", "ao_done"))
            if kk >= 3:
                w.append(("act", f"x2e_{j}_{kk - 3}"))
            elif j > 0:
                w.append(("act", f"x2e_{j - 1}_{13 + kk}"))
            tpose(bank[:, 0:128], xl2t[:, j % 2, 128 * kk:128 * (kk + 1)], idmf[:, :],
                  waits=w, mark=f"x2t_{j}_{kk}")
            act(lambda e, bank=bank, kk=kk, j=j:
                e.activation(xlnT2[:, kk, 128 * j:128 * (j + 1)], bank[:, 0:128],
                             AF.Copy),
                waits=[("pe", f"x2t_{j}_{kk}")], mark=f"x2e_{j}_{kk}")
        MK(f"x2p_{j}", "pe")
        MK(f"x2e_{j}", "act")
        if j == 2:
            dve(lambda e: e.tensor_tensor(sq16[:, :], amp[:, :], amp[:, :], ALU.mult),
                waits=[("dx", "consts_ld"), ("dve", "ao_done"), ("dve", "x1_1")],
                mark="mq1")
            dve(lambda e: e.tensor_tensor(p8[:, :], sq16[:, 0:8], sq16[:, 8:16],
                                          ALU.add), waits=[("dve", "mq1")], mark="mq2")
            dve(lambda e: e.reduce_sum(pcol[:, :], p8[:, :], axis=AX.X),
                waits=[("dve", "mq2")], mark="mq3")
            dve(lambda e: e.tensor_scalar(pcol[:, :], pcol[:, :], EPS_MOE, None,
                                          ALU.add), waits=[("dve", "mq3")], mark="mq4")
            dve(lambda e: e.reciprocal(pcol[:, :], pcol[:, :]),
                waits=[("dve", "mq4")], mark="mixrc")
            dve(lambda e: e.tensor_scalar(p8[:, :], p8[:, :], pcol[:, :], esc[:, :],
                                          ALU.mult, ALU.mult),
                waits=[("dve", "mixrc")], mark="mixp")
            dve(lambda e: e.tensor_copy(mixs[:, :], p8[:, :]),
                waits=[("dve", "mixp")], mark="mix_done")
        if j >= 2:
            rt_tile(j - 2)
    rt_tile(6)
    rt_tile(7)
    MKQ("xln2_done", "gp")

    dve(lambda e: e.tensor_copy(comb[:, :, :],
                                pb[3][:, 0:64].rearrange("p (a b) -> p a b", a=TT)),
        waits=[("pe", "cbm_7")], mark="comb_done")
    if debug:
        dma("sync", t_comb.ap(), comb[:, :, :].rearrange("p a b -> p (a b)"),
            waits=[("dve", "comb_done")])
        dma("sync", t_probs.ap().rearrange("(i p) d -> p i d", p=128), probs[:, :, :],
            waits=[("dve", "ctm_7")])

    dve(lambda e: e.memset(m_sb[:, :, :], 0.0), waits=[("dve", "comb_done")], mark="m0")
    for it2 in range(16):
        kslab, hf = divmod(it2, 2)
        wg = wga if it2 % 2 == 0 else wgb
        wu = wua if it2 % 2 == 0 else wub
        col = 1024 * kslab + 512 * hf
        w = [("pe", f"slu_{it2 - 2}")] if it2 >= 2 else [("dx", "xln2_done"),
                                                         ("pe", "x2p_7"),
                                                         ("dve", "ao_done")]
        dma("gp", wg[:, :, :],
            i_wg.ap()[:, col:col + 512].rearrange("(k p) c -> p k c", p=128), waits=w,
            mark=f"slwg_{it2}")
        dma("gp", wu[:, :, :],
            i_wu.ap()[:, col:col + 512].rearrange("(k p) c -> p k c", p=128),
            mark=f"slw_{it2}")
        for j in range(TT):
            gb = pb[2 * (j % 2)]
            ub = pb[2 * (j % 2) + 1]
            gg = it2 * 8 + j
            w2 = [("dw", f"slw_{it2}"), ("dw", f"slwg_{it2}"), ("act", f"x2e_{j}")]
            if gg >= 2:
                w2.append(("dve", f"bas_{gg - 2}"))
            else:
                w2.append(("act", "x2e_7"))
            mmg([(gb[:, 0:512], xlnT2[:, k, 128 * j:128 * (j + 1)], wg[:, k, :])
                 for k in range(KH)], waits=w2, mark=f"gm_{gg}")
            mmg([(ub[:, 0:512], xlnT2[:, k, 128 * j:128 * (j + 1)], wu[:, k, :])
                 for k in range(KH)], mark=f"um_{gg}")
            aw = [("pe", f"gm_{gg}")]
            if gg >= 2:
                aw.append(("dve", f"bas_{gg - 2}"))
            act(lambda e, gb=gb, gg=gg: e.activation(sg1[:, gg % 2, :], gb[:, 0:512],
                                                     AF.Silu),
                waits=aw, mark=f"sl_{gg}")
            dve(lambda e, ub=ub, gg=gg: e.tensor_tensor(bas1[:, gg % 2, :],
                                                        ub[:, 0:512], sg1[:, gg % 2, :],
                                                        ALU.mult),
                waits=[("act", f"sl_{gg}"), ("pe", f"um_{gg}")], mark=f"bt_{gg}")
            dve(lambda e, gg=gg, j=j, kslab=kslab, hf=hf:
                e.scalar_tensor_tensor(m_sb[:, j, 512 * hf:512 * (hf + 1)],
                                       bas1[:, gg % 2, :], comb[:, j, kslab:kslab + 1],
                                       m_sb[:, j, 512 * hf:512 * (hf + 1)],
                                       ALU.mult, ALU.add),
                waits=[("dve", "m0"), ("dve", f"bt_{gg}")], mark=f"bas_{gg}")
        MK(f"slu_{it2}", "pe")
    MK("m_final", "dve")
    dma("gp", h_m.ap().rearrange("(i p) d -> p i d", p=128), m_sb[:, :, :],
        waits=[("dve", "m_final")], mark="m_sp")
    for nb in range(4):
        wg = wga if nb % 2 == 0 else wgb
        wu = wua if nb % 2 == 0 else wub
        w = [("pe", f"shu_{nb - 2}")] if nb >= 2 else [("pe", f"slu_{14 + nb}")]
        dma("gp", wg[:, :, :],
            i_swg.ap()[:, 512 * nb:512 * (nb + 1)].rearrange("(k p) c -> p k c", p=128),
            waits=w, mark=f"shwg_{nb}")
        dma("gp", wu[:, :, :],
            i_swu.ap()[:, 512 * nb:512 * (nb + 1)].rearrange("(k p) c -> p k c", p=128),
            mark=f"shw_{nb}")
        for j in range(TT):
            gb = pb[2 * (j % 2)]
            ub = pb[2 * (j % 2) + 1]
            gg = nb * 8 + j
            w2 = [("dw", f"shw_{nb}"), ("dw", f"shwg_{nb}"), ("dw", "m_sp")]
            if gg >= 2:
                w2.append(("dve", f"sme_{gg - 2}"))
            mmg([(gb[:, 0:512], xlnT2[:, k, 128 * j:128 * (j + 1)], wg[:, k, :])
                 for k in range(KH)], waits=w2, mark=f"sgm_{gg}")
            mmg([(ub[:, 0:512], xlnT2[:, k, 128 * j:128 * (j + 1)], wu[:, k, :])
                 for k in range(KH)], mark=f"sum_{gg}")
            aw = [("pe", f"sgm_{gg}")]
            if gg >= 2:
                aw.append(("dve", f"sme_{gg - 2}"))
            act(lambda e, gb=gb, gg=gg: e.activation(sg1[:, gg % 2, :], gb[:, 0:512],
                                                     AF.Silu),
                waits=aw, mark=f"ss_{gg}")
            dve(lambda e, ub=ub, gg=gg, j=j, nb=nb:
                e.tensor_tensor(smid[:, j, 512 * nb:512 * (nb + 1)], ub[:, 0:512],
                                sg1[:, gg % 2, :], ALU.mult),
                waits=[("act", f"ss_{gg}"), ("pe", f"sum_{gg}")], mark=f"sme_{gg}")
        MK(f"shu_{nb}", "pe")
    MK("smid_done", "dve")
    for j in range(TT):
        dma("sync", h_smid.ap()[128 * j:128 * (j + 1), :], smid[:, j, :],
            waits=[("dve", f"sme_{24 + j}"), ("act", f"x2e_{j}")])
        if j == 3:
            MKQ("smid_spA", "sync")
    MKQ("smid_sp", "sync")
    for kk in range(8):
        dma("sync", mT[:, kk, :], h_m.ap()[:, 128 * kk:128 * (kk + 1)],
            transpose=True, waits=[("dw", "m_sp"), ("pe", "shu_3")], mark=f"mTp_{kk}")
    marks["mT_ld"] = [p for kk in range(8) for p in marks[f"mTp_{kk}"]]
    for kk in range(KH):
        dma("sync", smidT[:, kk, 0:512], h_smid.ap()[0:512, 128 * kk:128 * (kk + 1)],
            transpose=True, waits=[("dx", "smid_spA")], mark=f"smTa_{kk}")
    for kk in range(KH):
        dma("sync", smidT[:, kk, 512:TLOC],
            h_smid.ap()[512:TLOC, 128 * kk:128 * (kk + 1)],
            transpose=True, waits=[("dx", "smid_sp")], mark=f"smTb_{kk}")
    marks["smidT_ld"] = [p for kk in range(KH)
                         for p in marks[f"smTa_{kk}"] + marks[f"smTb_{kk}"]]

    for nb in range(4):
        wd_t = wdb_a if nb % 2 == 0 else wdb_b
        sw_t = swda if nb % 2 == 0 else swdb
        w = [("pe", f"dnu_{nb - 2}")] if nb >= 2 else [("dx", "smid_sp"),
                                                       ("pe", "shu_3")]
        dma("gp", wd_t[:, :, :],
            i_wd.ap()[:, 512 * nb:512 * (nb + 1)].rearrange("(k p) c -> p k c", p=128),
            waits=w, mark=f"dnwd_{nb}")
        dma("gp", sw_t[:, :, :],
            i_swd.ap()[:, 512 * nb:512 * (nb + 1)].rearrange("(k p) c -> p k c", p=128),
            mark=f"dnw_{nb}")
        for grp in range(2):
            gidx = nb * 2 + grp
            for j4 in range(4):
                j = grp * 4 + j4
                bank = pb[j4]
                w2 = [("dw", f"dnw_{nb}"), ("dw", f"dnwd_{nb}"), ("dx", "mT_ld"),
                      ("dx", "smidT_ld")]
                if gidx >= 1:
                    w2.append(("dve", f"dne_{gidx - 1}_{j4}"))
                lst = [(bank[:, 0:512], mT[:, k, 128 * j:128 * (j + 1)], wd_t[:, k, :])
                       for k in range(8)]
                lst += [(bank[:, 0:512], smidT[:, k, 128 * j:128 * (j + 1)], sw_t[:, k, :])
                        for k in range(KH)]
                mmg(lst, waits=w2, mark=f"dnm_{gidx}_{j4}")
            for j4 in range(4):
                j = grp * 4 + j4
                dve(lambda e, j=j, j4=j4, nb=nb, gidx=gidx:
                    e.tensor_copy(ffo[:, j, 512 * nb:512 * (nb + 1)], pb[j4][:, 0:512]),
                    waits=[("pe", f"dnm_{gidx}_{j4}")], mark=f"dne_{gidx}_{j4}")
        MK(f"dnu_{nb}", "pe")
    MK("down_all", "pe")
    MK("ffo_done", "dve")
    if debug:
        dma("sync", t_ffo.ap().rearrange("(i p) d -> p i d", p=128), ffo[:, :, :],
            waits=[("dve", "ffo_done")])
        dma("sync", t_cols.ap(), dcols[:, :, :].rearrange("p a b -> p (a b)"),
            waits=[("dve", "x1_7")])

    for j in range(TT):
        w = [("pe", "down_all"), ("dx", f"x1w_{j}")]
        if j >= 2:
            w.append(("dve", f"out_{j - 2}"))
        dma("sync", x1r[:, j % 2, :], h_x1.ap()[128 * j:128 * (j + 1), :], waits=w,
            mark=f"x1r_{j}")
        fw3 = [("dve", "ffo_done")]
        ln_stats(ffo[:, j, :], junk10[:, :], fw3, fw3, f"l3_{j}", cs=j % 2)
        rc0, bc0 = COLS[j % 2][4], COLS[j % 2][5]
        wq = [("dx", f"x1r_{j}"), ("dve", f"l3_{j}b")]
        if j >= 2:
            wq.append(("dx", f"outw_{j - 2}"))
        dve(lambda e, j=j, bc0=bc0: e.tensor_scalar(x2t[:, j % 2, :], x1r[:, j % 2, :],
                                                    bc0[:, :], None, ALU.add),
            waits=wq, mark=f"xo3_{j}")
        dve(lambda e, j=j, rc0=rc0: e.scalar_tensor_tensor(x2t[:, j % 2, :], ffo[:, j, :],
                                                           rc0[:, :], x2t[:, j % 2, :],
                                                           ALU.mult, ALU.add),
            waits=[("dve", f"xo3_{j}")], mark=f"out_{j}")
        MK(f"l3a_{j}", "dve")
        dma("sync", o_y.ap()[128 * j:128 * (j + 1), :], x2t[:, j % 2, :],
            waits=[("dve", f"out_{j}")], mark=f"outw_{j}")
    for i in range(NDMASEM):
        v = cnt[f"dx{i}"]
        if v > 0:
            OP("sync", lambda e, i=i, v=v: e.wait_ge(sh[f"dx{i}"], v), inc=None)

    def emit_for(engname):
        def body(eng):
            wm = {}
            for (en, fn, inc, waits) in ops:
                if en != engname:
                    continue
                for (s, v) in waits:
                    if isinstance(v, str):
                        pairs = marks[v]
                    else:
                        pairs = [(s, v)]
                    for (s2, vv) in pairs:
                        if vv <= 0:
                            continue
                        if vv > wm.get(s2, 0):
                            wm[s2] = vv
                            eng.wait_ge(sh[s2], vv)
                instr = fn(eng)
                if inc is not None and instr is not None:
                    instr.then_inc(sh[inc[0]], inc[1])
        return body

    with nc.Block() as block:
        block.sync(emit_for("sync"))
        block.gpsimd(emit_for("gp"))
        block.tensor(emit_for("pe"))
        block.vector(emit_for("dve"))
        block.scalar(emit_for("act"))
    return nc


_NC_CACHE = {}


def _get_nc():
    if "nc" not in _NC_CACHE:
        _NC_CACHE["nc"] = build_nc()
    return _NC_CACHE["nc"]


def _host_shared(inputs):
    f32 = np.float32
    g = {}
    g["wT"] = np.ascontiguousarray(np.asarray(inputs["L0_attn_in_proj_w"], f32).T)
    cw = np.asarray(inputs["L0_attn_conv_w"], f32)
    cd = np.zeros((NCT, 4, 128, 128), f32)
    idx = np.arange(128)
    for m in range(NCT):
        for k in range(4):
            cd[m, k, idx, idx] = cw[128 * m + idx, 0, k]
    g["cdiag"] = cd
    g["convb"] = np.asarray(inputs["L0_attn_conv_b"], f32).reshape(1, CONVD)
    g["alog"] = np.ascontiguousarray(np.asarray(inputs["L0_attn_A_log"], f32).reshape(HEADS, 1))
    g["dtb"] = np.broadcast_to(np.asarray(inputs["L0_attn_dt_bias"], f32), (128, HEADS)).copy()
    g["dd"] = np.broadcast_to(np.asarray(inputs["L0_attn_D"], f32), (128, HEADS)).copy()
    g["nw"] = np.broadcast_to(np.asarray(inputs["L0_attn_norm_w"], f32), (128, INTER)).copy()
    g["woT"] = np.ascontiguousarray(np.asarray(inputs["L0_attn_out_proj_w"], f32).T)
    g["wr"] = np.ascontiguousarray(np.asarray(inputs["L0_ffn_w_router"], f32))
    g["wg"] = np.ascontiguousarray(np.asarray(inputs["L0_ffn_w_gate"], f32))
    g["wu"] = np.ascontiguousarray(np.asarray(inputs["L0_ffn_w_up"], f32))
    g["wd"] = np.ascontiguousarray(np.asarray(inputs["L0_ffn_w_down"], f32))
    g["swg"] = np.ascontiguousarray(np.asarray(inputs["L0_ffn_shared_w_gate"], f32))
    g["swu"] = np.ascontiguousarray(np.asarray(inputs["L0_ffn_shared_w_up"], f32))
    g["swd"] = np.ascontiguousarray(np.asarray(inputs["L0_ffn_shared_w_down"], f32))
    a = np.asarray(inputs["L0_ffn_amplitudes"], f32)
    g["amp"] = np.ascontiguousarray(np.concatenate([a[:, :, 0], a[:, :, 1]], axis=1))
    g["esc"] = np.ascontiguousarray(np.asarray(inputs["L0_ffn_expert_scale"], f32).reshape(NE, 1))
    g["idmb"] = np.eye(128, dtype=ml_dtypes.bfloat16)
    g["idmf"] = np.eye(128, dtype=f32)
    tri = np.where(np.arange(128)[None, :] >= np.arange(128)[:, None], BIG, -BIG).astype(f32)
    g["dcapa"] = np.ascontiguousarray(np.concatenate([tri, np.full((128, 128), BIG, f32)], axis=1))
    g["dcapb"] = np.ascontiguousarray(tri)
    g["onesr"] = np.ones((1, 128), f32)
    sel = np.zeros((HEADS, HEADS * 128), f32)
    for hh in range(HEADS):
        sel[hh, hh * 128:(hh + 1) * 128] = 1.0
    g["selb"] = sel
    return g


def _host_percore(core, inputs):
    f32 = np.float32
    x = np.asarray(inputs["L0_attn_hidden_states"], f32)
    b, h = core // 2, core % 2
    xs = np.zeros((TPAD, HID), f32)
    xs[0:TLOC] = x[b, h * TLOC:(h + 1) * TLOC]
    if h == 1:
        xs[TLOC:TLOC + 3] = x[b, TLOC - 3:TLOC]
    mk = np.zeros((128, 8), f32)
    if h == 1:
        mk[:, core - 1] = 1.0
    return {"xin": xs, "maskv": mk}


def kernel(**inputs):
    f32 = np.float32
    nc = _get_nc()
    shared = _host_shared(inputs)
    in_maps = []
    for c in range(8):
        m = dict(shared)
        m.update(_host_percore(c, inputs))
        in_maps.append(m)
    res = run_bass_kernel_spmd(nc, in_maps, core_ids=list(range(8)))
    out = np.zeros((4, 2048, HID), f32)
    for c in range(8):
        b, h = c // 2, c % 2
        out[b, h * TLOC:(h + 1) * TLOC] = res.results[c]["yout"]
    return out
